# version 10
# baseline (speedup 1.0000x reference)
import contextlib
import math
import numpy as np
import ml_dtypes
import concourse.bass as bass
import concourse.mybir as mybir
from concourse.bass_utils import run_bass_kernel_spmd
from contextlib import ExitStack

F32 = mybir.dt.float32
BF16 = mybir.dt.bfloat16
I32 = mybir.dt.int32
AF = mybir.ActivationFunctionType
ALU = mybir.AluOpType
AX = mybir.AxisListType

S, D, H, HD = 2048, 4096, 32, 128
NQT = 8
NQ = NQT * 128
NKT = 16
QT_SETS = ([0, 3, 4, 7, 8, 11, 12, 15], [1, 2, 5, 6, 9, 10, 13, 14])
SHIFT = 18.0
SM_SCALE = 1.0 / math.sqrt(HD)
NTILE_LN = NQT + S // 128

_NC_CACHE = {}


class Ch:

    def __init__(self, nc, e, name, nslots):
        self.sems = [e(nc.semaphore(f"{name}{i}")) for i in range(nslots)]
        self.cnt = [0] * nslots

    def dma(self, eng, slot, out, in_, **kw):
        eng.dma_start(out=out, in_=in_, **kw).then_inc(self.sems[slot], 16)
        self.cnt[slot] += 16
        return self.cnt[slot]

    def wait(self, eng, slot, val=None):
        eng.wait_ge(self.sems[slot], self.cnt[slot] if val is None else val)

    def wait_all(self, eng):
        for i, s in enumerate(self.sems):
            if self.cnt[i]:
                eng.wait_ge(s, self.cnt[i])

    def totals(self):
        return list(self.cnt)


def build(debug=False, upto=6):
    nc = bass.Bass("TRN2", target_bir_lowering=False)

    x_in = nc.dram_tensor("x", [S, D], F32, kind="ExternalInput").ap()
    xq_in = nc.dram_tensor("xq", [NQ, D], F32, kind="ExternalInput").ap()
    wq_r = nc.dram_tensor("wq_r", [16, 32, 128, 256], F32, kind="ExternalInput").ap()
    wk_r = nc.dram_tensor("wk_r", [16, 32, 128, 256], F32, kind="ExternalInput").ap()
    wv_r = nc.dram_tensor("wv_r", [8, 32, 128, 512], F32, kind="ExternalInput").ap()
    wo_r = nc.dram_tensor("wo_r", [8, 32, 128, 512], F32, kind="ExternalInput").ap()
    wfT_r = nc.dram_tensor("wfT_r", [8, 32, 128, 512], I32, kind="ExternalInput").ap()
    bias_in = nc.dram_tensor("bias", [1, D], F32, kind="ExternalInput").ap()
    ws_in = nc.dram_tensor("ws", [1, 1], F32, kind="ExternalInput").ap()
    cosq_in = nc.dram_tensor("cosq", [128, NQ], F32, kind="ExternalInput").ap()
    sinq_in = nc.dram_tensor("sinq", [128, NQ], F32, kind="ExternalInput").ap()
    cosk_in = nc.dram_tensor("cosk", [128, S], F32, kind="ExternalInput").ap()
    sink_in = nc.dram_tensor("sink", [128, S], F32, kind="ExternalInput").ap()
    maskq_in = nc.dram_tensor("maskq", [NQT, 2, 128, 128], F32, kind="ExternalInput").ap()
    ident_in = nc.dram_tensor("ident", [128, 128], F32, kind="ExternalInput").ap()

    out_d = nc.dram_tensor("out", [NQ, D], F32, kind="ExternalOutput").ap()

    xnq_nat = nc.dram_tensor("xnq_nat", [NQ, D], BF16).ap()
    xn_nat = nc.dram_tensor("xn_nat", [S, D], BF16).ap()
    qT_scr = nc.dram_tensor("qT_scr", [H, 128, NQ], BF16).ap()
    kT_scr = nc.dram_tensor("kT_scr", [H, 128, S], BF16).ap()
    v_scr = nc.dram_tensor("v_scr", [H, NKT, 128, 128], BF16).ap()
    s_scr = nc.dram_tensor("s_scr", [NQ, D], BF16).ap()
    xnqT_scr = nc.dram_tensor("xnqT_scr", [128, 32 * NQ], BF16).ap()

    dbg = {}
    if debug:
        for name, shape in [("dbg_xn", [S, D]), ("dbg_xnq", [NQ, D]),
                            ("dbg_qt", [H * 128, NQ]), ("dbg_kt", [H * 128, S]),
                            ("dbg_v", [H * NKT * 128, 128]), ("dbg_s", [NQ, D]),
                            ("dbg_mv", [128, 3 * NTILE_LN])]:
            dbg[name] = nc.dram_tensor(name, shape, F32, kind="ExternalOutput").ap()

    class _StopEmit(Exception):
        pass

    with ExitStack() as ctx:
        e = ctx.enter_context

        sem_names = ["vi", "ti", "aln", "vln",
                     "tqk", "aqk", "vqk", "tvh",
                     "ts4", "tpv", "tz", "tzc", "tt1", "tt2",
                     "vm", "vraw", "vz", "vzi", "vnat", "aexp", "aoev",
                     "two", "vwo", "tff", "vwf", "aff", "vout", "hz"]
        SEM = {n: e(nc.semaphore(n)) for n in sem_names}
        CN = {n: 0 for n in sem_names}

        def inc(instr, name, n=1):
            instr.then_inc(SEM[name], n)
            CN[name] += n
            return instr

        c0 = Ch(nc, e, "c0", 3)
        ctq = Ch(nc, e, "ctq", 2)
        ctk = Ch(nc, e, "ctk", 2)
        cmsk = Ch(nc, e, "cmsk", 1)
        xin_c = Ch(nc, e, "xi", 2)
        xst = Ch(nc, e, "xs", 2)
        wsl = Ch(nc, e, "wl", 2)
        xbr = Ch(nc, e, "xb", 8)
        xbs = Ch(nc, e, "xv", 4)
        qst = Ch(nc, e, "qs", 8)
        athq = Ch(nc, e, "aq", 2)
        athk = Ch(nc, e, "ak", 2)
        athv = Ch(nc, e, "av", 2)
        xqs = Ch(nc, e, "xq", 2)
        sst = Ch(nc, e, "ss", 2)
        sti = Ch(nc, e, "st", 2)
        xti = Ch(nc, e, "xt", 2)
        ost = Ch(nc, e, "os", 2)
        cdb = Ch(nc, e, "db", 1)

        cpool = e(nc.sbuf_tensor("cpool", [128, 2], F32))
        ones_bf = e(nc.sbuf_tensor("ones_bf", [128, 1], BF16))
        ones128 = e(nc.sbuf_tensor("ones128", [128, 128], BF16))
        onesf_row = e(nc.sbuf_tensor("onesf_row", [1, 128], F32))
        ident_bf = e(nc.sbuf_tensor("ident_bf", [128, 128], BF16))
        biasrow_f = e(nc.sbuf_tensor("biasrow_f", [1, D], F32))
        biasrow_bf = e(nc.sbuf_tensor("biasrow_bf", [1, D], BF16))
        ws_sb = e(nc.sbuf_tensor("ws_sb", [1, 1], F32))
        ws_col = e(nc.sbuf_tensor("ws_col", [128, 1], F32))
        wsinv_f = e(nc.sbuf_tensor("wsinv_f", [1, 1], F32))
        wsinv_row_bf = e(nc.sbuf_tensor("wsinv_row_bf", [1, 128], BF16))
        statbuf = e(nc.sbuf_tensor("statbuf", [128, 8], F32))
        mvbuf = e(nc.sbuf_tensor("mvbuf", [128, 3 * NTILE_LN], F32))
        rsq = e(nc.sbuf_tensor("rsq", [128, NQT], F32))
        ucol = e(nc.sbuf_tensor("ucol", [128, NQT], F32))

        block = e(nc.Block())

        def w_slab_ap(w_ap, outer_idx, width):
            base = outer_idx * 32 * 128 * width
            return bass.AP(w_ap.tensor, base,
                           [[width, 128], [128 * width, 32], [1, width]])

        def emit_proj_T(w_r_ap, n_t, n_tb, cos_sb, sin_sb, scr_ap, slab,
                        ps, qraw, rotbuf, t2buf, xsrcT, xbr_chunks):
            t_base, a_base, v_base = CN["tqk"], CN["aqk"], CN["vqk"]
            qst_base = qst.totals()
            slab_v = {}

            def st_val(hc, tb):
                return qst_base[(hc % 2) * n_tb + tb] + 16 * (hc // 2 + 1)

            @block.gpsimd
            def _(g):
                for hcp in range(16):
                    if hcp >= 2:
                        g.wait_ge(SEM["tqk"], t_base + (2 * (hcp - 1)) * n_tb)
                    else:
                        g.wait_ge(SEM["tqk"], t_base)
                    slab_v[hcp] = wsl.dma(g, hcp % 2,
                                          slab[:, (hcp % 2) * 32 * 256:
                                               (hcp % 2 + 1) * 32 * 256],
                                          w_slab_ap(w_r_ap, hcp, 256))

            @block.tensor
            def _(t):
                for hc in range(32):
                    hcp = hc // 2
                    wsl.wait(t, hcp % 2, slab_v[hcp])
                    if hc >= 2:
                        t.wait_ge(SEM["aqk"], a_base + (hc - 1) * n_tb)
                    for ci in range(32):
                        if hc == 0:
                            sem_, val_ = xbr_chunks[ci]
                            t.wait_ge(sem_, val_)
                        lo = (hcp % 2) * 32 * 256 + ci * 256 + (hc % 2) * 128
                        lhsT = slab[:, lo:lo + 128]
                        for tb in range(n_tb):
                            mm = t.matmul(
                                ps[:, ((hc % 2) * n_tb + tb) * 512:
                                   ((hc % 2) * n_tb + tb + 1) * 512],
                                lhsT=lhsT,
                                rhs=xsrcT[:, ci * n_t + tb * 512:ci * n_t + tb * 512 + 512],
                                start=(ci == 0), stop=(ci == 31))
                            if ci == 31:
                                inc(mm, "tqk")

            @block.scalar
            def _(s):
                for hc in range(32):
                    for tb in range(n_tb):
                        s.wait_ge(SEM["tqk"], t_base + hc * n_tb + tb + 1)
                        slot = (hc % 2) * n_tb + tb
                        if hc >= 2:
                            qst.wait(s, slot, st_val(hc - 2, tb))
                        inc(s.copy(out=qraw[:, slot * 512:(slot + 1) * 512],
                                   in_=ps[:, slot * 512:(slot + 1) * 512]), "aqk")

            @block.vector
            def _(v):
                for hc in range(32):
                    for tb in range(n_tb):
                        v.wait_ge(SEM["aqk"], a_base + hc * n_tb + tb + 1)
                        slot = (hc % 2) * n_tb + tb
                        qr = qraw[:, slot * 512:(slot + 1) * 512]
                        c_sl = cos_sb[:, tb * 512:(tb + 1) * 512]
                        s_sl = sin_sb[:, tb * 512:(tb + 1) * 512]
                        inc(v.tensor_scalar_mul(out=rotbuf[0:64, :], in0=qr[64:128, :],
                                                scalar1=-1.0), "hz")
                        inc(v.tensor_copy(out=rotbuf[64:128, :], in_=qr[0:64, :]), "hz")
                        v.wait_ge(SEM["hz"], CN["hz"])
                        inc(v.tensor_mul(out=t2buf[:, :], in0=rotbuf[:, :], in1=s_sl), "hz")
                        inc(v.tensor_mul(out=qr, in0=qr, in1=c_sl), "hz")
                        v.wait_ge(SEM["hz"], CN["hz"])
                        inc(v.tensor_add(out=qr, in0=qr, in1=t2buf[:, :]), "vqk")

            @block.sync
            def _(sy):
                for hc in range(32):
                    for tb in range(n_tb):
                        sy.wait_ge(SEM["vqk"], v_base + hc * n_tb + tb + 1)
                        slot = (hc % 2) * n_tb + tb
                        dst = bass.AP(scr_ap.tensor, (hc * 128) * n_t + tb * 512,
                                      [[n_t, 128], [1, 512]])
                        got = qst.dma(sy, slot, dst,
                                      qraw[:, slot * 512:(slot + 1) * 512])
                        assert got == st_val(hc, tb), (hc, tb, got)


        with nc.sbuf_tensor("xqnT", [128, 32 * NQ], BF16) as xqnT, \
             nc.sbuf_tensor("slab2", [128, 2 * 32 * 256], BF16) as slab2, \
             nc.sbuf_tensor("cosq_sb", [128, NQ], BF16) as cosq_sb, \
             nc.sbuf_tensor("sinq_sb", [128, NQ], BF16) as sinq_sb, \
             nc.sbuf_tensor("qraw2", [128, 4 * 512], BF16) as qraw2, \
             nc.sbuf_tensor("rot2", [128, 512], BF16) as rot2, \
             nc.sbuf_tensor("t2b2", [128, 512], BF16) as t2b2, \
             nc.psum_tensor("ps2", [128, 4 * 512], F32) as ps2:
            with nc.sbuf_tensor("xin", [128, 2 * D], F32) as xin, \
                 nc.sbuf_tensor("xnout", [128, 2 * D], BF16) as xnout, \
                 nc.sbuf_tensor("sqdump", [128, D], F32) as sqdump, \
                 nc.psum_tensor("ps0", [128, 8], F32) as ps0:

                def ln_src(tt):
                    if tt < NQT:
                        return xq_in[tt * 128:(tt + 1) * 128, :]
                    t = tt - NQT
                    return x_in[t * 128:(t + 1) * 128, :]

                def ln_dst(tt):
                    if tt < NQT:
                        return xnq_nat[tt * 128:(tt + 1) * 128, :]
                    t = tt - NQT
                    return xn_nat[t * 128:(t + 1) * 128, :]

                ld = {}
                st_v = {}
                xnq_marks = [None]

                def aln_base(tt):
                    return 3 * min(tt, NQT) + 2 * max(0, tt - NQT)

                @block.sync
                def _(sy):
                    c0.dma(sy, 0, biasrow_f[:, :], bias_in)
                    c0.dma(sy, 1, ws_sb[:, :], ws_in)
                    for tt in range(NTILE_LN):
                        if tt >= 2:
                            sy.wait_ge(SEM["vln"], 2 * (tt - 2) + 2)
                        ld[tt] = xin_c.dma(sy, tt % 2,
                                           xin[:, (tt % 2) * D:(tt % 2 + 1) * D],
                                           ln_src(tt))
                        if tt >= 1:
                            sy.wait_ge(SEM["vln"], 2 * (tt - 1) + 2)
                            st_v[tt - 1] = xst.dma(sy, (tt - 1) % 2, ln_dst(tt - 1),
                                                   xnout[:, ((tt - 1) % 2) * D:
                                                         ((tt - 1) % 2 + 1) * D])
                            if tt - 1 == NQT - 1:
                                xnq_marks[0] = xst.totals()
                    sy.wait_ge(SEM["vln"], 2 * NTILE_LN)
                    st_v[NTILE_LN - 1] = xst.dma(
                        sy, (NTILE_LN - 1) % 2, ln_dst(NTILE_LN - 1),
                        xnout[:, ((NTILE_LN - 1) % 2) * D:((NTILE_LN - 1) % 2 + 1) * D])

                @block.gpsimd
                def _(g):
                    c0.dma(g, 2, ident_bf[:, :], ident_in)

                @block.vector
                def _(v):
                    v.memset(cpool[:, 0:1], -SHIFT)
                    v.memset(cpool[:, 1:2], 1e-5)
                    v.memset(ones_bf[:, :], 1.0)
                    v.memset(ones128[:, :], 1.0)
                    v.memset(mvbuf[:, :], 0.0)
                    inc(v.memset(onesf_row[:, :], 1.0), "vi")
                    c0.wait_all(v)
                    v.tensor_copy(out=biasrow_bf[:, :], in_=biasrow_f[:, :])
                    inc(v.reciprocal(out=wsinv_f[:, :], in_=ws_sb[:, :]), "hz")
                    v.wait_ge(SEM["hz"], CN["hz"])
                    v.tensor_scalar_mul(out=wsinv_row_bf[:, :], in0=onesf_row[:, :],
                                        scalar1=wsinv_f[0:1, 0:1])
                    v.wait_ge(SEM["ti"], 1)
                    inc(v.tensor_copy(out=ws_col[:, :], in_=ps0[:, 0:1]), "vi")
                    for tt in range(NTILE_LN):
                        xin_c.wait(v, tt % 2, ld[tt])
                        if tt >= 1:
                            v.wait_ge(SEM["vln"], 2 * tt)
                        sl = xin[:, (tt % 2) * D:(tt % 2 + 1) * D]
                        inc(v.reduce_sum(out=statbuf[:, 0:1], in_=sl, axis=AX.X), "hz")
                        v.wait_ge(SEM["hz"], CN["hz"])
                        v.wait_ge(SEM["aln"], aln_base(tt) + 1)
                        inc(v.tensor_scalar_mul(out=statbuf[:, 2:3], in0=statbuf[:, 0:1],
                                                scalar1=1.0 / D), "hz")
                        v.wait_ge(SEM["hz"], CN["hz"])
                        inc(v.tensor_mul(out=statbuf[:, 3:4], in0=statbuf[:, 2:3],
                                         in1=statbuf[:, 2:3]), "hz")
                        inc(v.tensor_scalar_mul(out=statbuf[:, 7:8], in0=statbuf[:, 1:2],
                                                scalar1=1.0 / D), "hz")
                        v.wait_ge(SEM["hz"], CN["hz"])
                        inc(v.tensor_sub(out=statbuf[:, 4:5], in0=statbuf[:, 7:8],
                                         in1=statbuf[:, 3:4]), "vln")
                        v.wait_ge(SEM["aln"], aln_base(tt) + 2)
                        inc(v.reciprocal(out=statbuf[:, 6:7], in_=statbuf[:, 5:6]), "hz")
                        v.wait_ge(SEM["hz"], CN["hz"])
                        if tt >= 2:
                            xst.wait(v, tt % 2, st_v[tt - 2])
                            if tt - 2 < NQT:
                                v.wait_ge(SEM["aln"], aln_base(tt - 2) + 3)
                        inc(v.tensor_scalar(out=xnout[:, (tt % 2) * D:(tt % 2 + 1) * D],
                                            in0=sl,
                                            scalar1=statbuf[:, 2:3],
                                            scalar2=statbuf[:, 6:7],
                                            op0=ALU.subtract, op1=ALU.mult), "vln")

                @block.tensor
                def _(t):
                    t.wait_ge(SEM["vi"], 1)
                    c0.wait_all(t)
                    inc(t.matmul(ps0[:, 0:1], lhsT=onesf_row[:, :], rhs=ws_sb[0:1, 0:1],
                                 start=True, stop=True), "ti")

                xbr_q_chunks = []

                @block.scalar
                def _(s):
                    s.wait_ge(SEM["vi"], 1)
                    for tt in range(NTILE_LN):
                        base = aln_base(tt)
                        xin_c.wait(s, tt % 2, ld[tt])
                        if tt >= 1:
                            s.wait_ge(SEM["aln"], base)
                            s.wait_ge(SEM["vln"], 2 * (tt - 1) + 1)
                        sl = xin[:, (tt % 2) * D:(tt % 2 + 1) * D]
                        inc(s.activation(out=sqdump[:, :], in_=sl, func=AF.Square,
                                         accum_out=statbuf[:, 1:2]), "aln")
                        s.wait_ge(SEM["vln"], 2 * tt + 1)
                        inc(s.activation(out=statbuf[:, 5:6], in_=statbuf[:, 4:5],
                                         func=AF.Sqrt, bias=cpool[:, 1:2]), "aln")
                        if tt < NQT:
                            s.wait_ge(SEM["vln"], 2 * tt + 2)
                            inc(s.activation(out=sqdump[:, :],
                                             in_=xnout[:, (tt % 2) * D:(tt % 2 + 1) * D],
                                             func=AF.Copy, accum_out=rsq[:, tt:tt + 1]),
                                "aln")
                        else:
                            for k in range(2):
                                ci = 2 * (tt - NQT) + k
                                if ci == 0:
                                    for i, sem in enumerate(xst.sems):
                                        if xnq_marks[0][i]:
                                            s.wait_ge(sem, xnq_marks[0][i])
                                xbs.wait(s, ci % 4)
                                val = xbs.dma(s, ci % 4,
                                              xqnT[:, ci * NQ:(ci + 1) * NQ],
                                              xnq_nat[0:NQ, ci * 128:(ci + 1) * 128],
                                              transpose=True)
                                xbr_q_chunks.append((xbs.sems[ci % 4], val))

            if upto >= 2:

                @block.gpsimd
                def _(g):
                    ctq.dma(g, 0, cosq_sb[:, :], cosq_in)
                    ctq.dma(g, 1, sinq_sb[:, :], sinq_in)

                xbs_q_tot = xbs.totals()

                @block.vector
                def _(v):
                    ctq.wait_all(v)

                emit_proj_T(wq_r, NQ, 2, cosq_sb, sinq_sb, qT_scr, slab2,
                            ps2, qraw2, rot2, t2b2, xqnT, xbr_q_chunks)

                @block.gpsimd
                def _(g):
                    for i, sem in enumerate(xbs.sems):
                        if xbs_q_tot[i]:
                            g.wait_ge(sem, xbs_q_tot[i])
                    for qt in range(NQT):
                        if qt >= 2:
                            xqs.wait(g, qt % 2)
                        src = bass.AP(xqnT, qt * 128,
                                      [[32 * NQ, 128], [NQ, 32], [1, 128]])
                        xqs.dma(g, qt % 2,
                                xnqT_scr[0:128, qt * 4096:(qt + 1) * 4096], src)
        P2_TQK = CN["tqk"]
        P2_AQK = CN["aqk"]

        try:
            if upto < 2:
                raise _StopEmit
            if upto < 3:
                raise _StopEmit
            with nc.sbuf_tensor("xnT", [128, 32 * S], BF16) as xnT, \
                 nc.sbuf_tensor("cosk_sb", [128, S], BF16) as cosk_sb, \
                 nc.sbuf_tensor("sink_sb", [128, S], BF16) as sink_sb:

                @block.gpsimd
                def _(g):
                    g.wait_ge(SEM["tqk"], P2_TQK)
                    qst.wait_all(g)
                    xst.wait_all(g)
                    xqs.wait_all(g)
                    ctk.dma(g, 0, cosk_sb[:, :], cosk_in)
                    ctk.dma(g, 1, sink_sb[:, :], sink_in)

                xbr_k_chunks = [None] * 32

                @block.sync
                def _(sy):
                    sy.wait_ge(SEM["tqk"], P2_TQK)
                    qst.wait_all(sy)
                    xst.wait_all(sy)
                    xqs.wait_all(sy)
                    for ci in range(32):
                        xbr.wait(sy, ci % 8)
                        val = xbr.dma(sy, ci % 8, xnT[:, ci * S:(ci + 1) * S],
                                      xn_nat[0:S, ci * 128:(ci + 1) * 128],
                                      transpose=True)
                        xbr_k_chunks[ci] = (xbr.sems[ci % 8], val)

                with nc.sbuf_tensor("slab3", [128, 2 * 32 * 256], BF16) as slab3, \
                     nc.sbuf_tensor("qraw3", [128, 8 * 512], BF16) as qraw3, \
                     nc.sbuf_tensor("rot3", [128, 512], BF16) as rot3, \
                     nc.sbuf_tensor("t2b3", [128, 512], BF16) as t2b3, \
                     nc.psum_tensor("ps3", [128, 8 * 512], F32) as ps3:

                    @block.vector
                    def _(v):
                        ctk.wait_all(v)

                    emit_proj_T(wk_r, S, 4, cosk_sb, sink_sb, kT_scr, slab3,
                                ps3, qraw3, rot3, t2b3, xnT, xbr_k_chunks)

                P3K_TQK = CN["tqk"]

                with nc.sbuf_tensor("vslab", [128, 32 * 512], BF16) as vslab, \
                     nc.sbuf_tensor("vev", [128, 8 * 512], BF16) as vev, \
                     nc.psum_tensor("psv", [128, 8 * 512], F32) as psv:

                    tv_base = CN["tqk"]
                    av_base = CN["aqk"]
                    tvh_base = CN["tvh"]
                    vq_base = qst.totals()
                    vslab_v = {}

                    def vst_val(halfidx, tt8):
                        return vq_base[tt8] + 16 * (halfidx + 1)

                    @block.gpsimd
                    def _(g):
                        qst.wait_all(g)
                        for vb in range(8):
                            for hf in range(2):
                                if hf == 0:
                                    if vb >= 1:
                                        g.wait_ge(SEM["tvh"], tvh_base + vb)
                                    else:
                                        g.wait_ge(SEM["tqk"], tv_base)
                                else:
                                    g.wait_ge(SEM["tqk"], tv_base + vb * 16)
                                src = bass.AP(wv_r.tensor,
                                              (vb * 32 + hf * 16) * 128 * 512,
                                              [[512, 128], [128 * 512, 16], [1, 512]])
                                vslab_v[(vb, hf)] = wsl.dma(
                                    g, hf, vslab[:, hf * 16 * 512:(hf + 1) * 16 * 512], src)

                    @block.tensor
                    def _(t):
                        for vb in range(8):
                            for half in range(2):
                                if half == 0:
                                    wsl.wait(t, 0, vslab_v[(vb, 0)])
                                if vb * 2 + half >= 1:
                                    t.wait_ge(SEM["aqk"], av_base + (vb * 2 + half) * 8)
                                for ci in range(32):
                                    if half == 0 and ci == 16:
                                        wsl.wait(t, 1, vslab_v[(vb, 1)])
                                    for tt8 in range(8):
                                        tt = half * 8 + tt8
                                        mm = t.matmul(
                                            psv[:, tt8 * 512:(tt8 + 1) * 512],
                                            lhsT=xnT[:, ci * S + tt * 128:
                                                     ci * S + tt * 128 + 128],
                                            rhs=vslab[:, ci * 512:(ci + 1) * 512],
                                            start=(ci == 0), stop=(ci == 31))
                                        if half == 1 and ci == 15 and tt8 == 7:
                                            inc(mm, "tvh")
                                        if ci == 31:
                                            inc(mm, "tqk")

                    @block.scalar
                    def _(s):
                        cnt = 0
                        for vb in range(8):
                            for half in range(2):
                                for tt8 in range(8):
                                    cnt += 1
                                    s.wait_ge(SEM["tqk"], tv_base + cnt)
                                    if vb * 2 + half >= 1:
                                        qst.wait(s, tt8, vst_val(vb * 2 + half - 1, tt8))
                                    inc(s.copy(out=vev[:, tt8 * 512:(tt8 + 1) * 512],
                                               in_=psv[:, tt8 * 512:(tt8 + 1) * 512]), "aqk")

                    @block.sync
                    def _(sy):
                        cnt = 0
                        for vb in range(8):
                            for half in range(2):
                                for tt8 in range(8):
                                    tt = half * 8 + tt8
                                    cnt += 1
                                    sy.wait_ge(SEM["aqk"], av_base + cnt)
                                    h0 = vb * 4
                                    dst = bass.AP(v_scr.tensor,
                                                  (h0 * NKT + tt) * 128 * 128,
                                                  [[128, 128], [NKT * 128 * 128, 4],
                                                   [1, 128]])
                                    got = qst.dma(
                                        sy, tt8, dst, vev[:, tt8 * 512:(tt8 + 1) * 512])
                                    assert got == vst_val(vb * 2 + half, tt8)
            P3_AQK = CN["aqk"]

            if upto < 4:
                raise _StopEmit
            with nc.sbuf_tensor("outhT", [128, H * NQ], BF16) as outhT:

                with nc.sbuf_tensor("mask_sb", [128, 16 * 128], BF16) as mask_sb, \
                     nc.sbuf_tensor("qh", [128, 2 * NQ], BF16) as qh, \
                     nc.sbuf_tensor("kh", [128, 2 * S], BF16) as kh, \
                     nc.sbuf_tensor("vh", [128, 2 * S], BF16) as vh, \
                     nc.sbuf_tensor("expT", [128, 2 * NQ], BF16) as expT, \
                     nc.sbuf_tensor("zinv_bc", [128, NQ], F32) as zinv_bc, \
                     nc.sbuf_tensor("pv_sb", [128, NQ], F32) as pv_sb, \
                     nc.sbuf_tensor("z_sb", [128, NQ], F32) as z_sb, \
                     nc.psum_tensor("psA", [128, 8 * 512], F32) as psA:
                    qv = {}
                    kv = {}
                    vv = {}

                    @block.gpsimd
                    def _(g):
                        g.wait_ge(SEM["aqk"], P3_AQK)
                        qst.wait_all(g)
                        msk = bass.AP(maskq_in.tensor, 0,
                                      [[128, 128], [128 * 128, 16], [1, 128]])
                        cmsk.dma(g, 0, mask_sb[:, :], msk)

                    @block.sync
                    def _(sy):
                        sy.wait_ge(SEM["aqk"], P3_AQK)
                        qst.wait_all(sy)
                        for h in range(H):
                            if h >= 2:
                                sy.wait_ge(SEM["vnat"], h - 1)
                            qv[h] = athq.dma(sy, h % 2,
                                             qh[:, (h % 2) * NQ:(h % 2 + 1) * NQ], qT_scr[h])
                            kv[h] = athk.dma(sy, h % 2,
                                             kh[:, (h % 2) * S:(h % 2 + 1) * S], kT_scr[h])
                            vsrc = bass.AP(v_scr.tensor, h * NKT * 128 * 128,
                                           [[128, 128], [128 * 128, NKT], [1, 128]])
                            vv[h] = athv.dma(sy, h % 2,
                                             vh[:, (h % 2) * S:(h % 2 + 1) * S], vsrc)

                    def strip_off(kt):
                        return (kt % 2) * NQ

                    def seg_list(kt):
                        lo = (kt // 2) * 128
                        out = []
                        for seg in range(0, NQ, 512):
                            a, b = max(seg, lo), seg + 512
                            if b > a:
                                out.append((a, b, seg))
                        return out

                    def last_kt_for_seg(seg):
                        last = -1
                        for kt in range(NKT):
                            if (kt // 2) * 128 < seg + 512:
                                last = kt
                        return last

                    LAST_KT = {seg: last_kt_for_seg(seg) for seg in range(0, NQ, 512)}

                    @block.tensor
                    def _(t):
                        for h in range(H):
                            athq.wait(t, h % 2, qv[h])
                            athk.wait(t, h % 2, kv[h])
                            athv.wait(t, h % 2, vv[h])
                            if h >= 1:
                                t.wait_ge(SEM["aexp"], h * NKT)
                                t.wait_ge(SEM["aoev"], 2 * h)
                            qsl = qh[:, (h % 2) * NQ:(h % 2 + 1) * NQ]
                            ksl = kh[:, (h % 2) * S:(h % 2 + 1) * S]
                            vsl = vh[:, (h % 2) * S:(h % 2 + 1) * S]

                            def scores(kt):
                                lo = (kt // 2) * 128
                                last = None
                                for sl0 in range(0, NQ - lo, 512):
                                    sl1 = min(sl0 + 512, NQ - lo)
                                    last = t.matmul(
                                        psA[:, strip_off(kt) + sl0:strip_off(kt) + sl1],
                                        lhsT=ksl[:, kt * 128:(kt + 1) * 128],
                                        rhs=qsl[:, lo + sl0:lo + sl1],
                                        start=True, stop=True)
                                inc(last, "ts4")

                            def zpv(kt):
                                lo = (kt // 2) * 128
                                mm = None
                                for (a, b, seg) in seg_list(kt):
                                    mm = t.matmul(
                                        psA[:, 3072 + a:3072 + b],
                                        lhsT=ones128[:, :],
                                        rhs=expT[:, strip_off(kt) + a - lo:
                                                 strip_off(kt) + b - lo],
                                        start=(kt == 0), stop=(kt == LAST_KT[seg]))
                                inc(mm, "tz")
                                for (a, b, seg) in seg_list(kt):
                                    mm = t.matmul(
                                        psA[:, 2048 + a:2048 + b],
                                        lhsT=vsl[:, kt * 128:(kt + 1) * 128],
                                        rhs=expT[:, strip_off(kt) + a - lo:
                                                 strip_off(kt) + b - lo],
                                        start=(kt == 0), stop=(kt == LAST_KT[seg]))
                                inc(mm, "tpv")

                            scores(0)
                            scores(1)
                            for kt in range(NKT):
                                t.wait_ge(SEM["aexp"], h * NKT + kt + 1)
                                zpv(kt)
                                if kt + 2 < NKT:
                                    scores(kt + 2)


                    @block.vector
                    def _(v):
                        cmsk.wait(v, 0)
                        for h in range(H):
                            for kt in range(NKT):
                                j0 = kt // 2
                                v.wait_ge(SEM["ts4"], h * NKT + kt + 1)
                                m_sl = mask_sb[:, (j0 * 2 + (kt % 2)) * 128:
                                               (j0 * 2 + (kt % 2) + 1) * 128]
                                inc(v.tensor_add(
                                    out=psA[:, strip_off(kt):strip_off(kt) + 128],
                                    in0=psA[:, strip_off(kt):strip_off(kt) + 128],
                                    in1=m_sl), "vm")
                            v.wait_ge(SEM["aoev"], 2 * h + 2)
                            inc(v.reciprocal(out=zinv_bc[:, :],
                                             in_=z_sb[:, :]), "hz")
                            v.wait_ge(SEM["hz"], CN["hz"])
                            inc(v.tensor_mul(out=outhT[:, h * NQ:(h + 1) * NQ],
                                             in0=pv_sb[:, :],
                                             in1=zinv_bc[:, :]), "vnat")

                    @block.scalar
                    def _(s):
                        for h in range(H):
                            for kt in range(NKT):
                                w = NQ - (kt // 2) * 128
                                s.wait_ge(SEM["vm"], h * NKT + kt + 1)
                                if kt >= 2:
                                    s.wait_ge(SEM["tpv"], h * NKT + kt - 1)
                                    s.wait_ge(SEM["tz"], h * NKT + kt - 1)
                                inc(s.activation(out=expT[:, strip_off(kt):strip_off(kt) + w],
                                                 in_=psA[:, strip_off(kt):strip_off(kt) + w],
                                                 func=AF.Exp, bias=cpool[:, 0:1],
                                                 scale=SM_SCALE), "aexp")
                            s.wait_ge(SEM["tpv"], (h + 1) * NKT)
                            if h >= 1:
                                s.wait_ge(SEM["vnat"], h)
                            inc(s.copy(out=pv_sb[:, :], in_=psA[:, 2048:3072]), "aoev")
                            s.wait_ge(SEM["tz"], (h + 1) * NKT)
                            inc(s.copy(out=z_sb[:, :], in_=psA[:, 3072:4096]), "aoev")


                if upto >= 5:
                    with nc.sbuf_tensor("woslab", [128, 2 * 32 * 512], BF16) as woslab, \
                         nc.sbuf_tensor("sbf", [128, 2 * 512], BF16) as sbf, \
                         nc.psum_tensor("ps5", [128, 2 * 512], F32) as ps5:

                        two0 = CN["two"]
                        vwo0 = CN["vwo"]
                        sst_base = sst.totals()
                        woslab_v = {}

                        def sst_val(idx):
                            return sst_base[idx % 2] + 16 * (idx // 2 + 1)

                        @block.gpsimd
                        def _(g):
                            g.wait_ge(SEM["vnat"], H)
                            for ob in range(8):
                                if ob >= 2:
                                    g.wait_ge(SEM["two"], two0 + (ob - 1) * 8)
                                woslab_v[ob] = wsl.dma(
                                    g, ob % 2,
                                    woslab[:, (ob % 2) * 32 * 512:(ob % 2 + 1) * 32 * 512],
                                    w_slab_ap(wo_r, ob, 512))

                        @block.tensor
                        def _(t):
                            t.wait_ge(SEM["vnat"], H)
                            for ob in range(8):
                                wsl.wait(t, ob % 2, woslab_v[ob])
                                for qt in range(NQT):
                                    idx = ob * 8 + qt
                                    if idx >= 2:
                                        t.wait_ge(SEM["vwo"], vwo0 + idx - 1)
                                    for ci in range(32):
                                        mm = t.matmul(
                                            ps5[:, (idx % 2) * 512:(idx % 2 + 1) * 512],
                                            lhsT=outhT[:, ci * NQ + qt * 128:
                                                       ci * NQ + qt * 128 + 128],
                                            rhs=woslab[:, (ob % 2) * 32 * 512 + ci * 512:
                                                       (ob % 2) * 32 * 512 + (ci + 1) * 512],
                                            start=(ci == 0), stop=(ci == 31))
                                    inc(mm, "two")

                        @block.vector
                        def _(v):
                            for idx in range(64):
                                v.wait_ge(SEM["two"], two0 + idx + 1)
                                if idx >= 2:
                                    sst.wait(v, idx % 2, sst_val(idx - 2))
                                inc(v.tensor_copy(
                                    out=sbf[:, (idx % 2) * 512:(idx % 2 + 1) * 512],
                                    in_=ps5[:, (idx % 2) * 512:(idx % 2 + 1) * 512]), "vwo")

                        @block.sync
                        def _(sy):
                            for ob in range(8):
                                for qt in range(NQT):
                                    idx = ob * 8 + qt
                                    sy.wait_ge(SEM["vwo"], vwo0 + idx + 1)
                                    got = sst.dma(
                                        sy, idx % 2,
                                        s_scr[qt * 128:(qt + 1) * 128, ob * 512:(ob + 1) * 512],
                                        sbf[:, (idx % 2) * 512:(idx % 2 + 1) * 512])
                                    assert got == sst_val(idx)

            P5_TWO = CN["two"]
            P5_VWO = CN["vwo"]

            if upto < 6:
                raise _StopEmit
            with nc.sbuf_tensor("xnqT6", [128, 32 * NQ], BF16) as xnqT6, \
                 nc.sbuf_tensor("wft", [128, 2 * 32 * 512], BF16) as wft, \
                 nc.sbuf_tensor("gbuf", [128, 2 * 512], BF16) as gbuf, \
                 nc.sbuf_tensor("stile", [128, 2 * 512], BF16) as stile, \
                 nc.sbuf_tensor("xtile", [128, 2 * 512], F32) as xtile, \
                 nc.sbuf_tensor("sgbuf", [128, 512], F32) as sgbuf, \
                 nc.sbuf_tensor("obuf", [128, 2 * 512], F32) as obuf, \
                 nc.psum_tensor("ps6", [128, 2 * 512], F32) as ps6:

                vwf0 = CN["vwf"]
                ost_base = ost.totals()
                tff0 = CN["tff"]
                aff0 = CN["aff"]
                vout0 = CN["vout"]
                sti_v = {}
                xti_v = {}
                ost_v = {}
                wft_v = {}
                xq6_marks = {}

                @block.gpsimd
                def _(g):
                    g.wait_ge(SEM["two"], P5_TWO)
                    g.wait_ge(SEM["vwo"], P5_VWO)
                    sst.wait_all(g)
                    for ob in range(8):
                        if ob >= 2:
                            g.wait_ge(SEM["tff"], tff0 + (ob - 1) * 8)
                        wft_v[ob] = wsl.dma(
                            g, ob % 2,
                            wft[:, (ob % 2) * 32 * 512:(ob % 2 + 1) * 32 * 512],
                            w_slab_ap(wfT_r, ob, 512))

                @block.sync
                def _(sy):
                    sy.wait_ge(SEM["two"], P5_TWO)
                    sy.wait_ge(SEM["vwo"], P5_VWO)
                    sst.wait_all(sy)
                    for qt in range(NQT):
                        xbr.wait(sy, qt % 8)
                        xbr.dma(sy, qt % 8,
                                xnqT6[:, qt * 4096:(qt + 1) * 4096],
                                xnqT_scr[0:128, qt * 4096:(qt + 1) * 4096])
                        xq6_marks[qt] = xbr.totals()
                    for ob_s in range(8):
                        for qt in range(NQT):
                            idx = ob_s * 8 + qt
                            sy.wait_ge(SEM["vout"], vout0 + idx + 1)
                            ost_v[idx] = ost.dma(
                                sy, idx % 2,
                                out_d[qt * 128:(qt + 1) * 128,
                                      ob_s * 512:(ob_s + 1) * 512],
                                obuf[:, (idx % 2) * 512:(idx % 2 + 1) * 512])
                    ost.wait_all(sy)

                @block.scalar
                def _(s):
                    s.wait_ge(SEM["two"], P5_TWO)
                    s.wait_ge(SEM["vwo"], P5_VWO)
                    sst.wait_all(s)
                    for ob in range(8):
                        for qt in range(NQT):
                            idx = ob * 8 + qt
                            if idx >= 2:
                                s.wait_ge(SEM["vout"], vout0 + idx - 1)
                            sti_v[idx] = sti.dma(
                                s, idx % 2,
                                stile[:, (idx % 2) * 512:(idx % 2 + 1) * 512],
                                s_scr[qt * 128:(qt + 1) * 128, ob * 512:(ob + 1) * 512])
                            xti_v[idx] = xti.dma(
                                s, idx % 2,
                                xtile[:, (idx % 2) * 512:(idx % 2 + 1) * 512],
                                xq_in[qt * 128:(qt + 1) * 128, ob * 512:(ob + 1) * 512])
                            s.wait_ge(SEM["tff"], tff0 + idx + 1)
                            if idx == 0:
                                s.wait_ge(SEM["vwf"], vwf0 + 1)
                            inc(s.activation(out=gbuf[:, (idx % 2) * 512:(idx % 2 + 1) * 512],
                                             in_=ps6[:, (idx % 2) * 512:(idx % 2 + 1) * 512],
                                             func=AF.Silu, scale=ws_col[:, 0:1],
                                             bias=ucol[:, qt:qt + 1]), "aff")

                @block.vector
                def _(v):
                    v.wait_ge(SEM["vi"], 2)
                    v.wait_ge(SEM["aln"], 3 * NQT)
                    inc(v.tensor_scalar_mul(out=ucol[:, :], in0=rsq[:, :],
                                            scalar1=ws_col[:, 0:1]), "hz")
                    v.wait_ge(SEM["hz"], CN["hz"])
                    inc(v.tensor_scalar_mul(out=ucol[:, :], in0=ucol[:, :],
                                            scalar1=-1.5), "vwf")
                    for ob in range(8):
                        for qt in range(NQT):
                            idx = ob * 8 + qt
                            v.wait_ge(SEM["aff"], aff0 + idx + 1)
                            sti.wait(v, idx % 2, sti_v[idx])
                            xti.wait(v, idx % 2, xti_v[idx])
                            if idx >= 2:
                                ost.wait(v, idx % 2,
                                         ost_base[idx % 2] + 16 * ((idx - 2) // 2 + 1))
                            if idx >= 1:
                                v.wait_ge(SEM["vout"], vout0 + idx)
                            inc(v.tensor_mul(out=sgbuf[:, :],
                                             in0=gbuf[:, (idx % 2) * 512:(idx % 2 + 1) * 512],
                                             in1=stile[:, (idx % 2) * 512:(idx % 2 + 1) * 512]),
                                "hz")
                            v.wait_ge(SEM["hz"], CN["hz"])
                            inc(v.tensor_add(
                                out=obuf[:, (idx % 2) * 512:(idx % 2 + 1) * 512],
                                in0=sgbuf[:, :],
                                in1=xtile[:, (idx % 2) * 512:(idx % 2 + 1) * 512]), "vout")

                @block.tensor
                def _(t):
                    t.wait_ge(SEM["vwo"], P5_VWO)
                    for ob in range(8):
                        wsl.wait(t, ob % 2, wft_v[ob])
                        for qt in range(NQT):
                            idx = ob * 8 + qt
                            if ob == 0:
                                for i, sem in enumerate(xbr.sems):
                                    t.wait_ge(sem, xq6_marks[qt][i])
                            if idx >= 2:
                                t.wait_ge(SEM["aff"], aff0 + idx - 1)
                            for ci in range(32):
                                t.matmul(ps6[:, (idx % 2) * 512:(idx % 2 + 1) * 512],
                                         lhsT=xnqT6[:, qt * 4096 + ci * 128:
                                                    qt * 4096 + (ci + 1) * 128],
                                         rhs=wft[:, (ob % 2) * 32 * 512 + ci * 512:
                                                 (ob % 2) * 32 * 512 + (ci + 1) * 512],
                                         start=(ci == 0), stop=False)
                            mm = t.matmul(ps6[:, (idx % 2) * 512:(idx % 2 + 1) * 512],
                                          lhsT=wsinv_row_bf[0:1, 0:128],
                                          rhs=biasrow_bf[0:1, ob * 512:(ob + 1) * 512],
                                          start=False, stop=True)
                            inc(mm, "tff")


        except _StopEmit:
            pass

        if debug:
            @block.gpsimd
            def _(g):
                for ch in [c0, ctq, ctk, cmsk, xin_c, xst, wsl, xbr, xbs, qst,
                           athq, athk, athv, xqs, sst, sti, xti, ost]:
                    ch.wait_all(g)
                for nm in ["vln", "vqk", "aqk", "aoev", "vwo", "vout"]:
                    if CN[nm]:
                        g.wait_ge(SEM[nm], CN[nm])
                taps = [(dbg["dbg_mv"], mvbuf[:, :]),
                        (dbg["dbg_xn"], xn_nat[:, :]),
                        (dbg["dbg_xnq"], xnq_nat[:, :])]
                if upto >= 2:
                    taps.append((dbg["dbg_qt"],
                                 bass.AP(qT_scr.tensor, 0, [[NQ, H * 128], [1, NQ]])))
                if upto >= 3:
                    taps.append((dbg["dbg_kt"],
                                 bass.AP(kT_scr.tensor, 0, [[S, H * 128], [1, S]])))
                    taps.append((dbg["dbg_v"],
                                 bass.AP(v_scr.tensor, 0,
                                         [[128, H * NKT * 128], [1, 128]])))
                if upto >= 5:
                    taps.append((dbg["dbg_s"], s_scr[:, :]))
                for dst, src in taps:
                    if cdb.cnt[0]:
                        cdb.wait(g, 0)
                    cdb.dma(g, 0, dst, src)
                cdb.wait_all(g)

    return nc



def _host_tables():
    inv = 1.0 / (10000.0 ** (np.arange(0, HD, 2, dtype=np.float64) / HD))
    pos = np.arange(S, dtype=np.float64)
    fr = np.outer(inv, pos)
    cosT = np.cos(np.concatenate([fr, fr], axis=0)).astype(np.float32)
    sinT = np.sin(np.concatenate([fr, fr], axis=0)).astype(np.float32)
    return cosT, sinT


def _relayout(w, n_outer, width):
    return np.ascontiguousarray(
        w.reshape(32, 128, n_outer, width).transpose(2, 0, 1, 3))


def _masks(qts):
    m = np.zeros((NQT, 2, 128, 128), dtype=np.float32)
    p = np.arange(128)
    ecols = np.arange(128)
    for j, qt in enumerate(qts):
        for i in range(2):
            kt = 2 * j + i
            valid = (kt * 128 + p[:, None]) <= (qt * 128 + ecols[None, :])
            m[j, i] = np.where(valid, 0.0, -1e30)
    return m


def kernel(L0_attn_hidden_states, L0_attn_wq, L0_attn_wk, L0_attn_wv, L0_attn_wo,
           L0_attn_kq1, L0_attn_kq2, L0_attn_kk1, L0_attn_kk2, L0_attn_scalingD,
           L0_attn_interaction_k, L0_attn_scalingD2, L0_ffn_weight_2bit,
           L0_ffn_weight_scale, L0_ffn_bias):
    x = np.ascontiguousarray(np.asarray(L0_attn_hidden_states, dtype=np.float32))
    wq = np.asarray(L0_attn_wq, dtype=np.float32)
    wk = np.asarray(L0_attn_wk, dtype=np.float32)
    wv = np.asarray(L0_attn_wv, dtype=np.float32)
    wo = np.asarray(L0_attn_wo, dtype=np.float32)
    wf = np.asarray(L0_ffn_weight_2bit, dtype=np.int32)
    ws = np.ascontiguousarray(np.asarray(L0_ffn_weight_scale, dtype=np.float32).reshape(1, 1))
    bias = np.ascontiguousarray(np.asarray(L0_ffn_bias, dtype=np.float32).reshape(1, D))

    cosT, sinT = _host_tables()
    ident = np.eye(128, dtype=np.float32)
    wq_r = _relayout(wq, 16, 256)
    wk_r = _relayout(wk, 16, 256)
    wv_r = _relayout(wv, 8, 512)
    wo_r = _relayout(wo, 8, 512)
    wfT_r = _relayout(wf.T, 8, 512)
    masks = [_masks(QT_SETS[0]), _masks(QT_SETS[1])]
    rows_sets = [np.concatenate([np.arange(qt * 128, (qt + 1) * 128) for qt in qs])
                 for qs in QT_SETS]

    in_maps = []
    for c in range(8):
        beta, par = c // 2, c % 2
        rows = rows_sets[par]
        in_maps.append({
            "x": np.ascontiguousarray(x[beta]),
            "xq": np.ascontiguousarray(x[beta][rows]),
            "wq_r": wq_r, "wk_r": wk_r, "wv_r": wv_r, "wo_r": wo_r,
            "wfT_r": wfT_r, "bias": bias, "ws": ws,
            "cosq": np.ascontiguousarray(cosT[:, rows]),
            "sinq": np.ascontiguousarray(sinT[:, rows]),
            "cosk": cosT, "sink": sinT,
            "maskq": masks[par], "ident": ident,
        })

    if "nc" not in _NC_CACHE:
        _NC_CACHE["nc"] = build(False)
    nc = _NC_CACHE["nc"]
    res = run_bass_kernel_spmd(nc, in_maps, core_ids=list(range(8)))

    out = np.empty((4, S, D), dtype=np.float32)
    for c in range(8):
        beta, par = c // 2, c % 2
        out[beta][rows_sets[par]] = np.asarray(res.results[c]["out"], dtype=np.float32)
    return out


# version 11
# speedup vs baseline: 1.0028x; 1.0028x over previous
import contextlib
import math
import numpy as np
import ml_dtypes
import concourse.bass as bass
import concourse.mybir as mybir
from concourse.bass_utils import run_bass_kernel_spmd
from contextlib import ExitStack

F32 = mybir.dt.float32
BF16 = mybir.dt.bfloat16
I32 = mybir.dt.int32
AF = mybir.ActivationFunctionType
ALU = mybir.AluOpType
AX = mybir.AxisListType

S, D, H, HD = 2048, 4096, 32, 128
NQT = 8
NQ = NQT * 128
NKT = 16
QT_SETS = ([0, 3, 4, 7, 8, 11, 12, 15], [1, 2, 5, 6, 9, 10, 13, 14])
SHIFT = 18.0
SM_SCALE = 1.0 / math.sqrt(HD)
NTILE_LN = NQT + S // 128

_NC_CACHE = {}


class Ch:

    def __init__(self, nc, e, name, nslots):
        self.sems = [e(nc.semaphore(f"{name}{i}")) for i in range(nslots)]
        self.cnt = [0] * nslots

    def dma(self, eng, slot, out, in_, **kw):
        eng.dma_start(out=out, in_=in_, **kw).then_inc(self.sems[slot], 16)
        self.cnt[slot] += 16
        return self.cnt[slot]

    def wait(self, eng, slot, val=None):
        eng.wait_ge(self.sems[slot], self.cnt[slot] if val is None else val)

    def wait_all(self, eng):
        for i, s in enumerate(self.sems):
            if self.cnt[i]:
                eng.wait_ge(s, self.cnt[i])

    def totals(self):
        return list(self.cnt)


def build(debug=False, upto=6):
    nc = bass.Bass("TRN2", target_bir_lowering=False)

    x_in = nc.dram_tensor("x", [S, D], F32, kind="ExternalInput").ap()
    xq_in = nc.dram_tensor("xq", [NQ, D], F32, kind="ExternalInput").ap()
    wq_r = nc.dram_tensor("wq_r", [16, 32, 128, 256], F32, kind="ExternalInput").ap()
    wk_r = nc.dram_tensor("wk_r", [16, 32, 128, 256], F32, kind="ExternalInput").ap()
    wv_r = nc.dram_tensor("wv_r", [8, 32, 128, 512], F32, kind="ExternalInput").ap()
    wo_r = nc.dram_tensor("wo_r", [8, 32, 128, 512], F32, kind="ExternalInput").ap()
    wfT_r = nc.dram_tensor("wfT_r", [8, 32, 128, 512], I32, kind="ExternalInput").ap()
    bias_in = nc.dram_tensor("bias", [1, D], F32, kind="ExternalInput").ap()
    ws_in = nc.dram_tensor("ws", [1, 1], F32, kind="ExternalInput").ap()
    cosq_in = nc.dram_tensor("cosq", [128, NQ], F32, kind="ExternalInput").ap()
    sinq_in = nc.dram_tensor("sinq", [128, NQ], F32, kind="ExternalInput").ap()
    cosk_in = nc.dram_tensor("cosk", [128, S], F32, kind="ExternalInput").ap()
    sink_in = nc.dram_tensor("sink", [128, S], F32, kind="ExternalInput").ap()
    maskq_in = nc.dram_tensor("maskq", [NQT, 2, 128, 128], F32, kind="ExternalInput").ap()
    ident_in = nc.dram_tensor("ident", [128, 128], F32, kind="ExternalInput").ap()

    out_d = nc.dram_tensor("out", [NQ, D], F32, kind="ExternalOutput").ap()

    xnq_nat = nc.dram_tensor("xnq_nat", [NQ, D], BF16).ap()
    xn_nat = nc.dram_tensor("xn_nat", [S, D], BF16).ap()
    qT_scr = nc.dram_tensor("qT_scr", [H, 128, NQ], BF16).ap()
    kT_scr = nc.dram_tensor("kT_scr", [H, 128, S], BF16).ap()
    v_scr = nc.dram_tensor("v_scr", [H, NKT, 128, 128], BF16).ap()
    s_scr = nc.dram_tensor("s_scr", [NQ, D], BF16).ap()
    xnqT_scr = nc.dram_tensor("xnqT_scr", [128, 32 * NQ], BF16).ap()

    dbg = {}
    if debug:
        for name, shape in [("dbg_xn", [S, D]), ("dbg_xnq", [NQ, D]),
                            ("dbg_qt", [H * 128, NQ]), ("dbg_kt", [H * 128, S]),
                            ("dbg_v", [H * NKT * 128, 128]), ("dbg_s", [NQ, D]),
                            ("dbg_mv", [128, 3 * NTILE_LN])]:
            dbg[name] = nc.dram_tensor(name, shape, F32, kind="ExternalOutput").ap()

    class _StopEmit(Exception):
        pass

    with ExitStack() as ctx:
        e = ctx.enter_context

        sem_names = ["vi", "ti", "aln", "vln",
                     "tqk", "aqk", "vqk", "tvh",
                     "ts4", "tpv", "tz", "tzc", "tt1", "tt2",
                     "vm", "vraw", "vz", "vzi", "vnat", "aexp", "aoev",
                     "two", "vwo", "tff", "vwf", "aff", "vout", "hz"]
        SEM = {n: e(nc.semaphore(n)) for n in sem_names}
        CN = {n: 0 for n in sem_names}

        def inc(instr, name, n=1):
            instr.then_inc(SEM[name], n)
            CN[name] += n
            return instr

        c0 = Ch(nc, e, "c0", 3)
        ctq = Ch(nc, e, "ctq", 2)
        ctk = Ch(nc, e, "ctk", 2)
        cmsk = Ch(nc, e, "cmsk", 1)
        xin_c = Ch(nc, e, "xi", 2)
        xst = Ch(nc, e, "xs", 2)
        wsl = Ch(nc, e, "wl", 2)
        xbr = Ch(nc, e, "xb", 8)
        xbs = Ch(nc, e, "xv", 4)
        qst = Ch(nc, e, "qs", 8)
        athq = Ch(nc, e, "aq", 2)
        athk = Ch(nc, e, "ak", 2)
        athv = Ch(nc, e, "av", 2)
        xqs = Ch(nc, e, "xq", 2)
        sst = Ch(nc, e, "ss", 2)
        sti = Ch(nc, e, "st", 2)
        xti = Ch(nc, e, "xt", 2)
        ost = Ch(nc, e, "os", 2)
        cdb = Ch(nc, e, "db", 1)

        cpool = e(nc.sbuf_tensor("cpool", [128, 2], F32))
        ones_bf = e(nc.sbuf_tensor("ones_bf", [128, 1], BF16))
        ones128 = e(nc.sbuf_tensor("ones128", [128, 128], BF16))
        onesf_row = e(nc.sbuf_tensor("onesf_row", [1, 128], F32))
        ident_bf = e(nc.sbuf_tensor("ident_bf", [128, 128], BF16))
        biasrow_f = e(nc.sbuf_tensor("biasrow_f", [1, D], F32))
        biasrow_bf = e(nc.sbuf_tensor("biasrow_bf", [1, D], BF16))
        ws_sb = e(nc.sbuf_tensor("ws_sb", [1, 1], F32))
        ws_col = e(nc.sbuf_tensor("ws_col", [128, 1], F32))
        wsinv_f = e(nc.sbuf_tensor("wsinv_f", [1, 1], F32))
        wsinv_row_bf = e(nc.sbuf_tensor("wsinv_row_bf", [1, 128], BF16))
        statbuf = e(nc.sbuf_tensor("statbuf", [128, 8], F32))
        mvbuf = e(nc.sbuf_tensor("mvbuf", [128, 3 * NTILE_LN], F32))
        rsq = e(nc.sbuf_tensor("rsq", [128, NQT], F32))
        ucol = e(nc.sbuf_tensor("ucol", [128, NQT], F32))

        block = e(nc.Block())

        def w_slab_ap(w_ap, outer_idx, width):
            base = outer_idx * 32 * 128 * width
            return bass.AP(w_ap.tensor, base,
                           [[width, 128], [128 * width, 32], [1, width]])

        def emit_proj_T(w_r_ap, n_t, n_tb, cos_sb, sin_sb, scr_ap, slab,
                        ps, qraw, rotbuf, t2buf, xsrcT, xbr_chunks):
            t_base, a_base, v_base = CN["tqk"], CN["aqk"], CN["vqk"]
            qst_base = qst.totals()
            slab_v = {}

            def st_val(hc, tb):
                return qst_base[(hc % 2) * n_tb + tb] + 16 * (hc // 2 + 1)

            @block.gpsimd
            def _(g):
                for hcp in range(16):
                    if hcp >= 2:
                        g.wait_ge(SEM["tqk"], t_base + (2 * (hcp - 1)) * n_tb)
                    else:
                        g.wait_ge(SEM["tqk"], t_base)
                    slab_v[hcp] = wsl.dma(g, hcp % 2,
                                          slab[:, (hcp % 2) * 32 * 256:
                                               (hcp % 2 + 1) * 32 * 256],
                                          w_slab_ap(w_r_ap, hcp, 256))

            @block.tensor
            def _(t):
                for hc in range(32):
                    hcp = hc // 2
                    wsl.wait(t, hcp % 2, slab_v[hcp])
                    if hc >= 2:
                        t.wait_ge(SEM["aqk"], a_base + (hc - 1) * n_tb)
                    for ci in range(32):
                        if hc == 0:
                            sem_, val_ = xbr_chunks[ci]
                            t.wait_ge(sem_, val_)
                        lo = (hcp % 2) * 32 * 256 + ci * 256 + (hc % 2) * 128
                        lhsT = slab[:, lo:lo + 128]
                        for tb in range(n_tb):
                            mm = t.matmul(
                                ps[:, ((hc % 2) * n_tb + tb) * 512:
                                   ((hc % 2) * n_tb + tb + 1) * 512],
                                lhsT=lhsT,
                                rhs=xsrcT[:, ci * n_t + tb * 512:ci * n_t + tb * 512 + 512],
                                start=(ci == 0), stop=(ci == 31))
                            if ci == 31:
                                inc(mm, "tqk")

            @block.scalar
            def _(s):
                for hc in range(32):
                    for tb in range(n_tb):
                        s.wait_ge(SEM["tqk"], t_base + hc * n_tb + tb + 1)
                        slot = (hc % 2) * n_tb + tb
                        if hc >= 2:
                            qst.wait(s, slot, st_val(hc - 2, tb))
                        inc(s.copy(out=qraw[:, slot * 512:(slot + 1) * 512],
                                   in_=ps[:, slot * 512:(slot + 1) * 512]), "aqk")

            @block.vector
            def _(v):
                for hc in range(32):
                    for tb in range(n_tb):
                        v.wait_ge(SEM["aqk"], a_base + hc * n_tb + tb + 1)
                        slot = (hc % 2) * n_tb + tb
                        qr = qraw[:, slot * 512:(slot + 1) * 512]
                        c_sl = cos_sb[:, tb * 512:(tb + 1) * 512]
                        s_sl = sin_sb[:, tb * 512:(tb + 1) * 512]
                        inc(v.tensor_scalar_mul(out=rotbuf[0:64, :], in0=qr[64:128, :],
                                                scalar1=-1.0), "hz")
                        inc(v.tensor_copy(out=rotbuf[64:128, :], in_=qr[0:64, :]), "hz")
                        v.wait_ge(SEM["hz"], CN["hz"])
                        inc(v.tensor_mul(out=t2buf[:, :], in0=rotbuf[:, :], in1=s_sl), "hz")
                        inc(v.tensor_mul(out=qr, in0=qr, in1=c_sl), "hz")
                        v.wait_ge(SEM["hz"], CN["hz"])
                        inc(v.tensor_add(out=qr, in0=qr, in1=t2buf[:, :]), "vqk")

            @block.sync
            def _(sy):
                for hc in range(32):
                    for tb in range(n_tb):
                        sy.wait_ge(SEM["vqk"], v_base + hc * n_tb + tb + 1)
                        slot = (hc % 2) * n_tb + tb
                        dst = bass.AP(scr_ap.tensor, (hc * 128) * n_t + tb * 512,
                                      [[n_t, 128], [1, 512]])
                        got = qst.dma(sy, slot, dst,
                                      qraw[:, slot * 512:(slot + 1) * 512])
                        assert got == st_val(hc, tb), (hc, tb, got)


        with nc.sbuf_tensor("xqnT", [128, 32 * NQ], BF16) as xqnT, \
             nc.sbuf_tensor("slab2", [128, 2 * 32 * 256], BF16) as slab2, \
             nc.sbuf_tensor("cosq_sb", [128, NQ], BF16) as cosq_sb, \
             nc.sbuf_tensor("sinq_sb", [128, NQ], BF16) as sinq_sb, \
             nc.sbuf_tensor("qraw2", [128, 4 * 512], BF16) as qraw2, \
             nc.sbuf_tensor("rot2", [128, 512], BF16) as rot2, \
             nc.sbuf_tensor("t2b2", [128, 512], BF16) as t2b2, \
             nc.psum_tensor("ps2", [128, 4 * 512], F32) as ps2:
            with nc.sbuf_tensor("xin", [128, 2 * D], F32) as xin, \
                 nc.sbuf_tensor("xnout", [128, 2 * D], BF16) as xnout, \
                 nc.sbuf_tensor("sqdump", [128, D], F32) as sqdump, \
                 nc.psum_tensor("ps0", [128, 8], F32) as ps0:

                def ln_src(tt):
                    if tt < NQT:
                        return xq_in[tt * 128:(tt + 1) * 128, :]
                    t = tt - NQT
                    return x_in[t * 128:(t + 1) * 128, :]

                def ln_dst(tt):
                    if tt < NQT:
                        return xnq_nat[tt * 128:(tt + 1) * 128, :]
                    t = tt - NQT
                    return xn_nat[t * 128:(t + 1) * 128, :]

                ld = {}
                st_v = {}
                xnq_marks = [None]

                def aln_base(tt):
                    return 3 * min(tt, NQT) + 2 * max(0, tt - NQT)

                @block.sync
                def _(sy):
                    c0.dma(sy, 0, biasrow_f[:, :], bias_in)
                    c0.dma(sy, 1, ws_sb[:, :], ws_in)
                    for tt in range(NTILE_LN):
                        if tt >= 2:
                            sy.wait_ge(SEM["vln"], 2 * (tt - 2) + 2)
                        ld[tt] = xin_c.dma(sy, tt % 2,
                                           xin[:, (tt % 2) * D:(tt % 2 + 1) * D],
                                           ln_src(tt))
                        if tt >= 1:
                            sy.wait_ge(SEM["vln"], 2 * (tt - 1) + 2)
                            st_v[tt - 1] = xst.dma(sy, (tt - 1) % 2, ln_dst(tt - 1),
                                                   xnout[:, ((tt - 1) % 2) * D:
                                                         ((tt - 1) % 2 + 1) * D])
                            if tt - 1 == NQT - 1:
                                xnq_marks[0] = xst.totals()
                    sy.wait_ge(SEM["vln"], 2 * NTILE_LN)
                    st_v[NTILE_LN - 1] = xst.dma(
                        sy, (NTILE_LN - 1) % 2, ln_dst(NTILE_LN - 1),
                        xnout[:, ((NTILE_LN - 1) % 2) * D:((NTILE_LN - 1) % 2 + 1) * D])

                @block.gpsimd
                def _(g):
                    c0.dma(g, 2, ident_bf[:, :], ident_in)

                @block.vector
                def _(v):
                    v.memset(cpool[:, 0:1], -SHIFT)
                    v.memset(cpool[:, 1:2], 1e-5)
                    v.memset(ones_bf[:, :], 1.0)
                    v.memset(ones128[:, :], 1.0)
                    v.memset(mvbuf[:, :], 0.0)
                    inc(v.memset(onesf_row[:, :], 1.0), "vi")
                    c0.wait_all(v)
                    v.tensor_copy(out=biasrow_bf[:, :], in_=biasrow_f[:, :])
                    inc(v.reciprocal(out=wsinv_f[:, :], in_=ws_sb[:, :]), "hz")
                    v.wait_ge(SEM["hz"], CN["hz"])
                    v.tensor_scalar_mul(out=wsinv_row_bf[:, :], in0=onesf_row[:, :],
                                        scalar1=wsinv_f[0:1, 0:1])
                    v.wait_ge(SEM["ti"], 1)
                    inc(v.tensor_copy(out=ws_col[:, :], in_=ps0[:, 0:1]), "vi")
                    for tt in range(NTILE_LN):
                        xin_c.wait(v, tt % 2, ld[tt])
                        if tt >= 1:
                            v.wait_ge(SEM["vln"], 2 * tt)
                        sl = xin[:, (tt % 2) * D:(tt % 2 + 1) * D]
                        inc(v.reduce_sum(out=statbuf[:, 0:1], in_=sl, axis=AX.X), "hz")
                        v.wait_ge(SEM["hz"], CN["hz"])
                        v.wait_ge(SEM["aln"], aln_base(tt) + 1)
                        inc(v.tensor_scalar_mul(out=statbuf[:, 2:3], in0=statbuf[:, 0:1],
                                                scalar1=1.0 / D), "hz")
                        v.wait_ge(SEM["hz"], CN["hz"])
                        inc(v.tensor_mul(out=statbuf[:, 3:4], in0=statbuf[:, 2:3],
                                         in1=statbuf[:, 2:3]), "hz")
                        inc(v.tensor_scalar_mul(out=statbuf[:, 7:8], in0=statbuf[:, 1:2],
                                                scalar1=1.0 / D), "hz")
                        v.wait_ge(SEM["hz"], CN["hz"])
                        inc(v.tensor_sub(out=statbuf[:, 4:5], in0=statbuf[:, 7:8],
                                         in1=statbuf[:, 3:4]), "vln")
                        v.wait_ge(SEM["aln"], aln_base(tt) + 2)
                        inc(v.reciprocal(out=statbuf[:, 6:7], in_=statbuf[:, 5:6]), "hz")
                        v.wait_ge(SEM["hz"], CN["hz"])
                        if tt >= 2:
                            xst.wait(v, tt % 2, st_v[tt - 2])
                            if tt - 2 < NQT:
                                v.wait_ge(SEM["aln"], aln_base(tt - 2) + 3)
                        inc(v.tensor_scalar(out=xnout[:, (tt % 2) * D:(tt % 2 + 1) * D],
                                            in0=sl,
                                            scalar1=statbuf[:, 2:3],
                                            scalar2=statbuf[:, 6:7],
                                            op0=ALU.subtract, op1=ALU.mult), "vln")

                @block.tensor
                def _(t):
                    t.wait_ge(SEM["vi"], 1)
                    c0.wait_all(t)
                    inc(t.matmul(ps0[:, 0:1], lhsT=onesf_row[:, :], rhs=ws_sb[0:1, 0:1],
                                 start=True, stop=True), "ti")

                xbr_q_chunks = []

                @block.scalar
                def _(s):
                    s.wait_ge(SEM["vi"], 1)
                    for tt in range(NTILE_LN):
                        base = aln_base(tt)
                        xin_c.wait(s, tt % 2, ld[tt])
                        if tt >= 1:
                            s.wait_ge(SEM["aln"], base)
                            s.wait_ge(SEM["vln"], 2 * (tt - 1) + 1)
                        sl = xin[:, (tt % 2) * D:(tt % 2 + 1) * D]
                        inc(s.activation(out=sqdump[:, :], in_=sl, func=AF.Square,
                                         accum_out=statbuf[:, 1:2]), "aln")
                        s.wait_ge(SEM["vln"], 2 * tt + 1)
                        inc(s.activation(out=statbuf[:, 5:6], in_=statbuf[:, 4:5],
                                         func=AF.Sqrt, bias=cpool[:, 1:2]), "aln")
                        if tt < NQT:
                            s.wait_ge(SEM["vln"], 2 * tt + 2)
                            inc(s.activation(out=sqdump[:, :],
                                             in_=xnout[:, (tt % 2) * D:(tt % 2 + 1) * D],
                                             func=AF.Copy, accum_out=rsq[:, tt:tt + 1]),
                                "aln")
                        else:
                            for k in range(2):
                                ci = 2 * (tt - NQT) + k
                                if ci == 0:
                                    for i, sem in enumerate(xst.sems):
                                        if xnq_marks[0][i]:
                                            s.wait_ge(sem, xnq_marks[0][i])
                                xbs.wait(s, ci % 4)
                                val = xbs.dma(s, ci % 4,
                                              xqnT[:, ci * NQ:(ci + 1) * NQ],
                                              xnq_nat[0:NQ, ci * 128:(ci + 1) * 128],
                                              transpose=True)
                                xbr_q_chunks.append((xbs.sems[ci % 4], val))

            if upto >= 2:

                @block.gpsimd
                def _(g):
                    ctq.dma(g, 0, cosq_sb[:, :], cosq_in)
                    ctq.dma(g, 1, sinq_sb[:, :], sinq_in)

                xbs_q_tot = xbs.totals()

                @block.vector
                def _(v):
                    ctq.wait_all(v)

                emit_proj_T(wq_r, NQ, 2, cosq_sb, sinq_sb, qT_scr, slab2,
                            ps2, qraw2, rot2, t2b2, xqnT, xbr_q_chunks)

                @block.gpsimd
                def _(g):
                    for i, sem in enumerate(xbs.sems):
                        if xbs_q_tot[i]:
                            g.wait_ge(sem, xbs_q_tot[i])
                    for qt in range(NQT):
                        if qt >= 2:
                            xqs.wait(g, qt % 2)
                        src = bass.AP(xqnT, qt * 128,
                                      [[32 * NQ, 128], [NQ, 32], [1, 128]])
                        xqs.dma(g, qt % 2,
                                xnqT_scr[0:128, qt * 4096:(qt + 1) * 4096], src)
        P2_TQK = CN["tqk"]
        P2_AQK = CN["aqk"]

        try:
            if upto < 2:
                raise _StopEmit
            if upto < 3:
                raise _StopEmit
            with nc.sbuf_tensor("xnT", [128, 32 * S], BF16) as xnT, \
                 nc.sbuf_tensor("cosk_sb", [128, S], BF16) as cosk_sb, \
                 nc.sbuf_tensor("sink_sb", [128, S], BF16) as sink_sb:

                @block.gpsimd
                def _(g):
                    g.wait_ge(SEM["tqk"], P2_TQK)
                    qst.wait_all(g)
                    xst.wait_all(g)
                    xqs.wait_all(g)
                    ctk.dma(g, 0, cosk_sb[:, :], cosk_in)
                    ctk.dma(g, 1, sink_sb[:, :], sink_in)

                xbr_k_chunks = [None] * 32

                @block.sync
                def _(sy):
                    sy.wait_ge(SEM["tqk"], P2_TQK)
                    qst.wait_all(sy)
                    xst.wait_all(sy)
                    xqs.wait_all(sy)
                    for ci in range(32):
                        xbr.wait(sy, ci % 8)
                        val = xbr.dma(sy, ci % 8, xnT[:, ci * S:(ci + 1) * S],
                                      xn_nat[0:S, ci * 128:(ci + 1) * 128],
                                      transpose=True)
                        xbr_k_chunks[ci] = (xbr.sems[ci % 8], val)

                with nc.sbuf_tensor("slab3", [128, 2 * 32 * 256], BF16) as slab3, \
                     nc.sbuf_tensor("qraw3", [128, 8 * 512], BF16) as qraw3, \
                     nc.sbuf_tensor("rot3", [128, 512], BF16) as rot3, \
                     nc.sbuf_tensor("t2b3", [128, 512], BF16) as t2b3, \
                     nc.psum_tensor("ps3", [128, 8 * 512], F32) as ps3:

                    @block.vector
                    def _(v):
                        ctk.wait_all(v)

                    emit_proj_T(wk_r, S, 4, cosk_sb, sink_sb, kT_scr, slab3,
                                ps3, qraw3, rot3, t2b3, xnT, xbr_k_chunks)

                P3K_TQK = CN["tqk"]

                with nc.sbuf_tensor("vslab", [128, 32 * 512], BF16) as vslab, \
                     nc.sbuf_tensor("vev", [128, 8 * 512], BF16) as vev, \
                     nc.psum_tensor("psv", [128, 8 * 512], F32) as psv:

                    tv_base = CN["tqk"]
                    av_base = CN["aqk"]
                    tvh_base = CN["tvh"]
                    vq_base = qst.totals()
                    vslab_v = {}

                    def vst_val(halfidx, tt8):
                        return vq_base[tt8] + 16 * (halfidx + 1)

                    @block.gpsimd
                    def _(g):
                        qst.wait_all(g)
                        for vb in range(8):
                            for hf in range(2):
                                if hf == 0:
                                    if vb >= 1:
                                        g.wait_ge(SEM["tvh"], tvh_base + vb)
                                    else:
                                        g.wait_ge(SEM["tqk"], tv_base)
                                else:
                                    g.wait_ge(SEM["tqk"], tv_base + vb * 16)
                                src = bass.AP(wv_r.tensor,
                                              (vb * 32 + hf * 16) * 128 * 512,
                                              [[512, 128], [128 * 512, 16], [1, 512]])
                                vslab_v[(vb, hf)] = wsl.dma(
                                    g, hf, vslab[:, hf * 16 * 512:(hf + 1) * 16 * 512], src)

                    @block.tensor
                    def _(t):
                        for vb in range(8):
                            for half in range(2):
                                if half == 0:
                                    wsl.wait(t, 0, vslab_v[(vb, 0)])
                                if vb * 2 + half >= 1:
                                    t.wait_ge(SEM["aqk"], av_base + (vb * 2 + half) * 8)
                                for ci in range(32):
                                    if half == 0 and ci == 16:
                                        wsl.wait(t, 1, vslab_v[(vb, 1)])
                                    for tt8 in range(8):
                                        tt = half * 8 + tt8
                                        mm = t.matmul(
                                            psv[:, tt8 * 512:(tt8 + 1) * 512],
                                            lhsT=xnT[:, ci * S + tt * 128:
                                                     ci * S + tt * 128 + 128],
                                            rhs=vslab[:, ci * 512:(ci + 1) * 512],
                                            start=(ci == 0), stop=(ci == 31))
                                        if half == 1 and ci == 15 and tt8 == 7:
                                            inc(mm, "tvh")
                                        if ci == 31:
                                            inc(mm, "tqk")

                    @block.scalar
                    def _(s):
                        cnt = 0
                        for vb in range(8):
                            for half in range(2):
                                for tt8 in range(8):
                                    cnt += 1
                                    s.wait_ge(SEM["tqk"], tv_base + cnt)
                                    if vb * 2 + half >= 1:
                                        qst.wait(s, tt8, vst_val(vb * 2 + half - 1, tt8))
                                    inc(s.copy(out=vev[:, tt8 * 512:(tt8 + 1) * 512],
                                               in_=psv[:, tt8 * 512:(tt8 + 1) * 512]), "aqk")

                    @block.sync
                    def _(sy):
                        cnt = 0
                        for vb in range(8):
                            for half in range(2):
                                for tt8 in range(8):
                                    tt = half * 8 + tt8
                                    cnt += 1
                                    sy.wait_ge(SEM["aqk"], av_base + cnt)
                                    h0 = vb * 4
                                    dst = bass.AP(v_scr.tensor,
                                                  (h0 * NKT + tt) * 128 * 128,
                                                  [[128, 128], [NKT * 128 * 128, 4],
                                                   [1, 128]])
                                    got = qst.dma(
                                        sy, tt8, dst, vev[:, tt8 * 512:(tt8 + 1) * 512])
                                    assert got == vst_val(vb * 2 + half, tt8)
            P3_AQK = CN["aqk"]

            if upto < 4:
                raise _StopEmit
            with nc.sbuf_tensor("outhT", [128, H * NQ], BF16) as outhT:

                with nc.sbuf_tensor("mask_sb", [128, 16 * 128], BF16) as mask_sb, \
                     nc.sbuf_tensor("qh", [128, 2 * NQ], BF16) as qh, \
                     nc.sbuf_tensor("kh", [128, 2 * S], BF16) as kh, \
                     nc.sbuf_tensor("vh", [128, 2 * S], BF16) as vh, \
                     nc.sbuf_tensor("expT", [128, 2 * NQ], BF16) as expT, \
                     nc.sbuf_tensor("zinv_bc", [128, NQ], F32) as zinv_bc, \
                     nc.sbuf_tensor("pv_sb", [128, 2 * NQ], F32) as pv_sb, \
                     nc.sbuf_tensor("z_sb", [128, 2 * NQ], F32) as z_sb, \
                     nc.psum_tensor("psA", [128, 8 * 512], F32) as psA:
                    qv = {}
                    kv = {}
                    vv = {}

                    @block.gpsimd
                    def _(g):
                        g.wait_ge(SEM["aqk"], P3_AQK)
                        qst.wait_all(g)
                        msk = bass.AP(maskq_in.tensor, 0,
                                      [[128, 128], [128 * 128, 16], [1, 128]])
                        cmsk.dma(g, 0, mask_sb[:, :], msk)

                    @block.sync
                    def _(sy):
                        sy.wait_ge(SEM["aqk"], P3_AQK)
                        qst.wait_all(sy)
                        for h in range(H):
                            if h >= 2:
                                sy.wait_ge(SEM["vnat"], h - 1)
                            qv[h] = athq.dma(sy, h % 2,
                                             qh[:, (h % 2) * NQ:(h % 2 + 1) * NQ], qT_scr[h])
                            kv[h] = athk.dma(sy, h % 2,
                                             kh[:, (h % 2) * S:(h % 2 + 1) * S], kT_scr[h])
                            vsrc = bass.AP(v_scr.tensor, h * NKT * 128 * 128,
                                           [[128, 128], [128 * 128, NKT], [1, 128]])
                            vv[h] = athv.dma(sy, h % 2,
                                             vh[:, (h % 2) * S:(h % 2 + 1) * S], vsrc)

                    def strip_off(kt):
                        return (kt % 2) * NQ

                    def seg_list(kt):
                        lo = (kt // 2) * 128
                        out = []
                        for seg in range(0, NQ, 512):
                            a, b = max(seg, lo), seg + 512
                            if b > a:
                                out.append((a, b, seg))
                        return out

                    def last_kt_for_seg(seg):
                        last = -1
                        for kt in range(NKT):
                            if (kt // 2) * 128 < seg + 512:
                                last = kt
                        return last

                    LAST_KT = {seg: last_kt_for_seg(seg) for seg in range(0, NQ, 512)}

                    @block.tensor
                    def _(t):
                        for h in range(H):
                            athq.wait(t, h % 2, qv[h])
                            athk.wait(t, h % 2, kv[h])
                            athv.wait(t, h % 2, vv[h])
                            if h >= 1:
                                t.wait_ge(SEM["aexp"], h * NKT)
                                t.wait_ge(SEM["aoev"], 2 * h)
                            qsl = qh[:, (h % 2) * NQ:(h % 2 + 1) * NQ]
                            ksl = kh[:, (h % 2) * S:(h % 2 + 1) * S]
                            vsl = vh[:, (h % 2) * S:(h % 2 + 1) * S]

                            def scores(kt):
                                lo = (kt // 2) * 128
                                last = None
                                for sl0 in range(0, NQ - lo, 512):
                                    sl1 = min(sl0 + 512, NQ - lo)
                                    last = t.matmul(
                                        psA[:, strip_off(kt) + sl0:strip_off(kt) + sl1],
                                        lhsT=ksl[:, kt * 128:(kt + 1) * 128],
                                        rhs=qsl[:, lo + sl0:lo + sl1],
                                        start=True, stop=True)
                                inc(last, "ts4")

                            def zpv(kt):
                                lo = (kt // 2) * 128
                                mm = None
                                for (a, b, seg) in seg_list(kt):
                                    mm = t.matmul(
                                        psA[:, 3072 + a:3072 + b],
                                        lhsT=ones128[:, :],
                                        rhs=expT[:, strip_off(kt) + a - lo:
                                                 strip_off(kt) + b - lo],
                                        start=(kt == 0), stop=(kt == LAST_KT[seg]))
                                inc(mm, "tz")
                                for (a, b, seg) in seg_list(kt):
                                    mm = t.matmul(
                                        psA[:, 2048 + a:2048 + b],
                                        lhsT=vsl[:, kt * 128:(kt + 1) * 128],
                                        rhs=expT[:, strip_off(kt) + a - lo:
                                                 strip_off(kt) + b - lo],
                                        start=(kt == 0), stop=(kt == LAST_KT[seg]))
                                inc(mm, "tpv")

                            scores(0)
                            scores(1)
                            for kt in range(NKT):
                                t.wait_ge(SEM["aexp"], h * NKT + kt + 1)
                                zpv(kt)
                                if kt + 2 < NKT:
                                    scores(kt + 2)


                    @block.vector
                    def _(v):
                        cmsk.wait(v, 0)

                        def post_step(hp, j):
                            if hp < 0:
                                return
                            if j == 0:
                                v.wait_ge(SEM["aoev"], 2 * hp + 2)
                                if hp >= 1:
                                    v.wait_ge(SEM["vnat"], hp)
                            if j < NQT:
                                inc(v.reciprocal(
                                    out=zinv_bc[:, j * 128:(j + 1) * 128],
                                    in_=z_sb[:, (hp % 2) * NQ + j * 128:
                                             (hp % 2) * NQ + (j + 1) * 128]), "hz")
                            else:
                                v.wait_ge(SEM["hz"], CN["hz"])
                                inc(v.tensor_mul(out=outhT[:, hp * NQ:(hp + 1) * NQ],
                                                 in0=pv_sb[:, (hp % 2) * NQ:
                                                           (hp % 2 + 1) * NQ],
                                                 in1=zinv_bc[:, :]), "vnat")

                        for h in range(H):
                            for kt in range(NKT):
                                j0 = kt // 2
                                v.wait_ge(SEM["ts4"], h * NKT + kt + 1)
                                m_sl = mask_sb[:, (j0 * 2 + (kt % 2)) * 128:
                                               (j0 * 2 + (kt % 2) + 1) * 128]
                                inc(v.tensor_add(
                                    out=psA[:, strip_off(kt):strip_off(kt) + 128],
                                    in0=psA[:, strip_off(kt):strip_off(kt) + 128],
                                    in1=m_sl), "vm")
                                if kt % 2 == 1:
                                    post_step(h - 1, kt // 2)
                            post_step(h - 1, NQT)
                        for j in range(NQT + 1):
                            post_step(H - 1, j)

                    @block.scalar
                    def _(s):
                        for h in range(H):
                            for kt in range(NKT):
                                w = NQ - (kt // 2) * 128
                                s.wait_ge(SEM["vm"], h * NKT + kt + 1)
                                if kt >= 2:
                                    s.wait_ge(SEM["tpv"], h * NKT + kt - 1)
                                    s.wait_ge(SEM["tz"], h * NKT + kt - 1)
                                inc(s.activation(out=expT[:, strip_off(kt):strip_off(kt) + w],
                                                 in_=psA[:, strip_off(kt):strip_off(kt) + w],
                                                 func=AF.Exp, bias=cpool[:, 0:1],
                                                 scale=SM_SCALE), "aexp")
                            s.wait_ge(SEM["tpv"], (h + 1) * NKT)
                            if h >= 2:
                                s.wait_ge(SEM["vnat"], h - 1)
                            inc(s.copy(out=pv_sb[:, (h % 2) * NQ:(h % 2 + 1) * NQ],
                                       in_=psA[:, 2048:3072]), "aoev")
                            s.wait_ge(SEM["tz"], (h + 1) * NKT)
                            inc(s.copy(out=z_sb[:, (h % 2) * NQ:(h % 2 + 1) * NQ],
                                       in_=psA[:, 3072:4096]), "aoev")


                if upto >= 5:
                    with nc.sbuf_tensor("woslab", [128, 2 * 32 * 512], BF16) as woslab, \
                         nc.sbuf_tensor("sbf", [128, 2 * 512], BF16) as sbf, \
                         nc.psum_tensor("ps5", [128, 2 * 512], F32) as ps5:

                        two0 = CN["two"]
                        vwo0 = CN["vwo"]
                        sst_base = sst.totals()
                        woslab_v = {}

                        def sst_val(idx):
                            return sst_base[idx % 2] + 16 * (idx // 2 + 1)

                        @block.gpsimd
                        def _(g):
                            g.wait_ge(SEM["vnat"], H)
                            for ob in range(8):
                                if ob >= 2:
                                    g.wait_ge(SEM["two"], two0 + (ob - 1) * 8)
                                woslab_v[ob] = wsl.dma(
                                    g, ob % 2,
                                    woslab[:, (ob % 2) * 32 * 512:(ob % 2 + 1) * 32 * 512],
                                    w_slab_ap(wo_r, ob, 512))

                        @block.tensor
                        def _(t):
                            t.wait_ge(SEM["vnat"], H)
                            for ob in range(8):
                                wsl.wait(t, ob % 2, woslab_v[ob])
                                for qt in range(NQT):
                                    idx = ob * 8 + qt
                                    if idx >= 2:
                                        t.wait_ge(SEM["vwo"], vwo0 + idx - 1)
                                    for ci in range(32):
                                        mm = t.matmul(
                                            ps5[:, (idx % 2) * 512:(idx % 2 + 1) * 512],
                                            lhsT=outhT[:, ci * NQ + qt * 128:
                                                       ci * NQ + qt * 128 + 128],
                                            rhs=woslab[:, (ob % 2) * 32 * 512 + ci * 512:
                                                       (ob % 2) * 32 * 512 + (ci + 1) * 512],
                                            start=(ci == 0), stop=(ci == 31))
                                    inc(mm, "two")

                        @block.vector
                        def _(v):
                            for idx in range(64):
                                v.wait_ge(SEM["two"], two0 + idx + 1)
                                if idx >= 2:
                                    sst.wait(v, idx % 2, sst_val(idx - 2))
                                inc(v.tensor_copy(
                                    out=sbf[:, (idx % 2) * 512:(idx % 2 + 1) * 512],
                                    in_=ps5[:, (idx % 2) * 512:(idx % 2 + 1) * 512]), "vwo")

                        @block.sync
                        def _(sy):
                            for ob in range(8):
                                for qt in range(NQT):
                                    idx = ob * 8 + qt
                                    sy.wait_ge(SEM["vwo"], vwo0 + idx + 1)
                                    got = sst.dma(
                                        sy, idx % 2,
                                        s_scr[qt * 128:(qt + 1) * 128, ob * 512:(ob + 1) * 512],
                                        sbf[:, (idx % 2) * 512:(idx % 2 + 1) * 512])
                                    assert got == sst_val(idx)

            P5_TWO = CN["two"]
            P5_VWO = CN["vwo"]

            if upto < 6:
                raise _StopEmit
            with nc.sbuf_tensor("xnqT6", [128, 32 * NQ], BF16) as xnqT6, \
                 nc.sbuf_tensor("wft", [128, 2 * 32 * 512], BF16) as wft, \
                 nc.sbuf_tensor("gbuf", [128, 2 * 512], BF16) as gbuf, \
                 nc.sbuf_tensor("stile", [128, 2 * 512], BF16) as stile, \
                 nc.sbuf_tensor("xtile", [128, 2 * 512], F32) as xtile, \
                 nc.sbuf_tensor("sgbuf", [128, 512], F32) as sgbuf, \
                 nc.sbuf_tensor("obuf", [128, 2 * 512], F32) as obuf, \
                 nc.psum_tensor("ps6", [128, 2 * 512], F32) as ps6:

                vwf0 = CN["vwf"]
                ost_base = ost.totals()
                tff0 = CN["tff"]
                aff0 = CN["aff"]
                vout0 = CN["vout"]
                sti_v = {}
                xti_v = {}
                ost_v = {}
                wft_v = {}
                xq6_marks = {}

                @block.gpsimd
                def _(g):
                    g.wait_ge(SEM["two"], P5_TWO)
                    g.wait_ge(SEM["vwo"], P5_VWO)
                    sst.wait_all(g)
                    for ob in range(8):
                        if ob >= 2:
                            g.wait_ge(SEM["tff"], tff0 + (ob - 1) * 8)
                        wft_v[ob] = wsl.dma(
                            g, ob % 2,
                            wft[:, (ob % 2) * 32 * 512:(ob % 2 + 1) * 32 * 512],
                            w_slab_ap(wfT_r, ob, 512))

                @block.sync
                def _(sy):
                    sy.wait_ge(SEM["two"], P5_TWO)
                    sy.wait_ge(SEM["vwo"], P5_VWO)
                    sst.wait_all(sy)
                    for qt in range(NQT):
                        xbr.wait(sy, qt % 8)
                        xbr.dma(sy, qt % 8,
                                xnqT6[:, qt * 4096:(qt + 1) * 4096],
                                xnqT_scr[0:128, qt * 4096:(qt + 1) * 4096])
                        xq6_marks[qt] = xbr.totals()
                    for ob_s in range(8):
                        for qt in range(NQT):
                            idx = ob_s * 8 + qt
                            sy.wait_ge(SEM["vout"], vout0 + idx + 1)
                            ost_v[idx] = ost.dma(
                                sy, idx % 2,
                                out_d[qt * 128:(qt + 1) * 128,
                                      ob_s * 512:(ob_s + 1) * 512],
                                obuf[:, (idx % 2) * 512:(idx % 2 + 1) * 512])
                    ost.wait_all(sy)

                @block.scalar
                def _(s):
                    s.wait_ge(SEM["two"], P5_TWO)
                    s.wait_ge(SEM["vwo"], P5_VWO)
                    sst.wait_all(s)
                    for ob in range(8):
                        for qt in range(NQT):
                            idx = ob * 8 + qt
                            if idx >= 2:
                                s.wait_ge(SEM["vout"], vout0 + idx - 1)
                            sti_v[idx] = sti.dma(
                                s, idx % 2,
                                stile[:, (idx % 2) * 512:(idx % 2 + 1) * 512],
                                s_scr[qt * 128:(qt + 1) * 128, ob * 512:(ob + 1) * 512])
                            xti_v[idx] = xti.dma(
                                s, idx % 2,
                                xtile[:, (idx % 2) * 512:(idx % 2 + 1) * 512],
                                xq_in[qt * 128:(qt + 1) * 128, ob * 512:(ob + 1) * 512])
                            s.wait_ge(SEM["tff"], tff0 + idx + 1)
                            if idx == 0:
                                s.wait_ge(SEM["vwf"], vwf0 + 1)
                            inc(s.activation(out=gbuf[:, (idx % 2) * 512:(idx % 2 + 1) * 512],
                                             in_=ps6[:, (idx % 2) * 512:(idx % 2 + 1) * 512],
                                             func=AF.Silu, scale=ws_col[:, 0:1],
                                             bias=ucol[:, qt:qt + 1]), "aff")

                @block.vector
                def _(v):
                    v.wait_ge(SEM["vi"], 2)
                    v.wait_ge(SEM["aln"], 3 * NQT)
                    inc(v.tensor_scalar_mul(out=ucol[:, :], in0=rsq[:, :],
                                            scalar1=ws_col[:, 0:1]), "hz")
                    v.wait_ge(SEM["hz"], CN["hz"])
                    inc(v.tensor_scalar_mul(out=ucol[:, :], in0=ucol[:, :],
                                            scalar1=-1.5), "vwf")
                    for ob in range(8):
                        for qt in range(NQT):
                            idx = ob * 8 + qt
                            v.wait_ge(SEM["aff"], aff0 + idx + 1)
                            sti.wait(v, idx % 2, sti_v[idx])
                            xti.wait(v, idx % 2, xti_v[idx])
                            if idx >= 2:
                                ost.wait(v, idx % 2,
                                         ost_base[idx % 2] + 16 * ((idx - 2) // 2 + 1))
                            if idx >= 1:
                                v.wait_ge(SEM["vout"], vout0 + idx)
                            inc(v.tensor_mul(out=sgbuf[:, :],
                                             in0=gbuf[:, (idx % 2) * 512:(idx % 2 + 1) * 512],
                                             in1=stile[:, (idx % 2) * 512:(idx % 2 + 1) * 512]),
                                "hz")
                            v.wait_ge(SEM["hz"], CN["hz"])
                            inc(v.tensor_add(
                                out=obuf[:, (idx % 2) * 512:(idx % 2 + 1) * 512],
                                in0=sgbuf[:, :],
                                in1=xtile[:, (idx % 2) * 512:(idx % 2 + 1) * 512]), "vout")

                @block.tensor
                def _(t):
                    t.wait_ge(SEM["vwo"], P5_VWO)
                    for ob in range(8):
                        wsl.wait(t, ob % 2, wft_v[ob])
                        for qt in range(NQT):
                            idx = ob * 8 + qt
                            if ob == 0:
                                for i, sem in enumerate(xbr.sems):
                                    t.wait_ge(sem, xq6_marks[qt][i])
                            if idx >= 2:
                                t.wait_ge(SEM["aff"], aff0 + idx - 1)
                            for ci in range(32):
                                t.matmul(ps6[:, (idx % 2) * 512:(idx % 2 + 1) * 512],
                                         lhsT=xnqT6[:, qt * 4096 + ci * 128:
                                                    qt * 4096 + (ci + 1) * 128],
                                         rhs=wft[:, (ob % 2) * 32 * 512 + ci * 512:
                                                 (ob % 2) * 32 * 512 + (ci + 1) * 512],
                                         start=(ci == 0), stop=False)
                            mm = t.matmul(ps6[:, (idx % 2) * 512:(idx % 2 + 1) * 512],
                                          lhsT=wsinv_row_bf[0:1, 0:128],
                                          rhs=biasrow_bf[0:1, ob * 512:(ob + 1) * 512],
                                          start=False, stop=True)
                            inc(mm, "tff")


        except _StopEmit:
            pass

        if debug:
            @block.gpsimd
            def _(g):
                for ch in [c0, ctq, ctk, cmsk, xin_c, xst, wsl, xbr, xbs, qst,
                           athq, athk, athv, xqs, sst, sti, xti, ost]:
                    ch.wait_all(g)
                for nm in ["vln", "vqk", "aqk", "aoev", "vwo", "vout"]:
                    if CN[nm]:
                        g.wait_ge(SEM[nm], CN[nm])
                taps = [(dbg["dbg_mv"], mvbuf[:, :]),
                        (dbg["dbg_xn"], xn_nat[:, :]),
                        (dbg["dbg_xnq"], xnq_nat[:, :])]
                if upto >= 2:
                    taps.append((dbg["dbg_qt"],
                                 bass.AP(qT_scr.tensor, 0, [[NQ, H * 128], [1, NQ]])))
                if upto >= 3:
                    taps.append((dbg["dbg_kt"],
                                 bass.AP(kT_scr.tensor, 0, [[S, H * 128], [1, S]])))
                    taps.append((dbg["dbg_v"],
                                 bass.AP(v_scr.tensor, 0,
                                         [[128, H * NKT * 128], [1, 128]])))
                if upto >= 5:
                    taps.append((dbg["dbg_s"], s_scr[:, :]))
                for dst, src in taps:
                    if cdb.cnt[0]:
                        cdb.wait(g, 0)
                    cdb.dma(g, 0, dst, src)
                cdb.wait_all(g)

    return nc



def _host_tables():
    inv = 1.0 / (10000.0 ** (np.arange(0, HD, 2, dtype=np.float64) / HD))
    pos = np.arange(S, dtype=np.float64)
    fr = np.outer(inv, pos)
    cosT = np.cos(np.concatenate([fr, fr], axis=0)).astype(np.float32)
    sinT = np.sin(np.concatenate([fr, fr], axis=0)).astype(np.float32)
    return cosT, sinT


def _relayout(w, n_outer, width):
    return np.ascontiguousarray(
        w.reshape(32, 128, n_outer, width).transpose(2, 0, 1, 3))


def _masks(qts):
    m = np.zeros((NQT, 2, 128, 128), dtype=np.float32)
    p = np.arange(128)
    ecols = np.arange(128)
    for j, qt in enumerate(qts):
        for i in range(2):
            kt = 2 * j + i
            valid = (kt * 128 + p[:, None]) <= (qt * 128 + ecols[None, :])
            m[j, i] = np.where(valid, 0.0, -1e30)
    return m


def kernel(L0_attn_hidden_states, L0_attn_wq, L0_attn_wk, L0_attn_wv, L0_attn_wo,
           L0_attn_kq1, L0_attn_kq2, L0_attn_kk1, L0_attn_kk2, L0_attn_scalingD,
           L0_attn_interaction_k, L0_attn_scalingD2, L0_ffn_weight_2bit,
           L0_ffn_weight_scale, L0_ffn_bias):
    x = np.ascontiguousarray(np.asarray(L0_attn_hidden_states, dtype=np.float32))
    wq = np.asarray(L0_attn_wq, dtype=np.float32)
    wk = np.asarray(L0_attn_wk, dtype=np.float32)
    wv = np.asarray(L0_attn_wv, dtype=np.float32)
    wo = np.asarray(L0_attn_wo, dtype=np.float32)
    wf = np.asarray(L0_ffn_weight_2bit, dtype=np.int32)
    ws = np.ascontiguousarray(np.asarray(L0_ffn_weight_scale, dtype=np.float32).reshape(1, 1))
    bias = np.ascontiguousarray(np.asarray(L0_ffn_bias, dtype=np.float32).reshape(1, D))

    cosT, sinT = _host_tables()
    ident = np.eye(128, dtype=np.float32)
    wq_r = _relayout(wq, 16, 256)
    wk_r = _relayout(wk, 16, 256)
    wv_r = _relayout(wv, 8, 512)
    wo_r = _relayout(wo, 8, 512)
    wfT_r = _relayout(wf.T, 8, 512)
    masks = [_masks(QT_SETS[0]), _masks(QT_SETS[1])]
    rows_sets = [np.concatenate([np.arange(qt * 128, (qt + 1) * 128) for qt in qs])
                 for qs in QT_SETS]

    in_maps = []
    for c in range(8):
        beta, par = c // 2, c % 2
        rows = rows_sets[par]
        in_maps.append({
            "x": np.ascontiguousarray(x[beta]),
            "xq": np.ascontiguousarray(x[beta][rows]),
            "wq_r": wq_r, "wk_r": wk_r, "wv_r": wv_r, "wo_r": wo_r,
            "wfT_r": wfT_r, "bias": bias, "ws": ws,
            "cosq": np.ascontiguousarray(cosT[:, rows]),
            "sinq": np.ascontiguousarray(sinT[:, rows]),
            "cosk": cosT, "sink": sinT,
            "maskq": masks[par], "ident": ident,
        })

    if "nc" not in _NC_CACHE:
        _NC_CACHE["nc"] = build(False)
    nc = _NC_CACHE["nc"]
    res = run_bass_kernel_spmd(nc, in_maps, core_ids=list(range(8)))

    out = np.empty((4, S, D), dtype=np.float32)
    for c in range(8):
        beta, par = c // 2, c % 2
        out[beta][rows_sets[par]] = np.asarray(res.results[c]["out"], dtype=np.float32)
    return out


# version 12
# speedup vs baseline: 1.0352x; 1.0323x over previous
import contextlib
import math
import numpy as np
import ml_dtypes
import concourse.bass as bass
import concourse.mybir as mybir
from concourse.bass_utils import run_bass_kernel_spmd
from contextlib import ExitStack

F32 = mybir.dt.float32
BF16 = mybir.dt.bfloat16
I32 = mybir.dt.int32
AF = mybir.ActivationFunctionType
ALU = mybir.AluOpType
AX = mybir.AxisListType

S, D, H, HD = 2048, 4096, 32, 128
NQT = 8
NQ = NQT * 128
NKT = 16
QT_SETS = ([0, 3, 4, 7, 8, 11, 12, 15], [1, 2, 5, 6, 9, 10, 13, 14])
SHIFT = 18.0
SM_SCALE = 1.0 / math.sqrt(HD)
NTILE_LN = NQT + S // 128

_NC_CACHE = {}


class Ch:

    def __init__(self, nc, e, name, nslots):
        self.sems = [e(nc.semaphore(f"{name}{i}")) for i in range(nslots)]
        self.cnt = [0] * nslots

    def dma(self, eng, slot, out, in_, **kw):
        eng.dma_start(out=out, in_=in_, **kw).then_inc(self.sems[slot], 16)
        self.cnt[slot] += 16
        return self.cnt[slot]

    def wait(self, eng, slot, val=None):
        eng.wait_ge(self.sems[slot], self.cnt[slot] if val is None else val)

    def wait_all(self, eng):
        for i, s in enumerate(self.sems):
            if self.cnt[i]:
                eng.wait_ge(s, self.cnt[i])

    def totals(self):
        return list(self.cnt)


def build(debug=False, upto=6):
    nc = bass.Bass("TRN2", target_bir_lowering=False)

    x_in = nc.dram_tensor("x", [S, D], F32, kind="ExternalInput").ap()
    xq_in = nc.dram_tensor("xq", [NQ, D], F32, kind="ExternalInput").ap()
    wq_r = nc.dram_tensor("wq_r", [16, 32, 128, 256], F32, kind="ExternalInput").ap()
    wk_r = nc.dram_tensor("wk_r", [16, 32, 128, 256], F32, kind="ExternalInput").ap()
    wv_r = nc.dram_tensor("wv_r", [8, 32, 128, 512], F32, kind="ExternalInput").ap()
    wo_r = nc.dram_tensor("wo_r", [8, 32, 128, 512], F32, kind="ExternalInput").ap()
    wfT_r = nc.dram_tensor("wfT_r", [8, 32, 128, 512], I32, kind="ExternalInput").ap()
    bias_in = nc.dram_tensor("bias", [1, D], F32, kind="ExternalInput").ap()
    ws_in = nc.dram_tensor("ws", [1, 1], F32, kind="ExternalInput").ap()
    cosq_in = nc.dram_tensor("cosq", [128, NQ], F32, kind="ExternalInput").ap()
    sinq_in = nc.dram_tensor("sinq", [128, NQ], F32, kind="ExternalInput").ap()
    cosk_in = nc.dram_tensor("cosk", [128, S], F32, kind="ExternalInput").ap()
    sink_in = nc.dram_tensor("sink", [128, S], F32, kind="ExternalInput").ap()
    maskq_in = nc.dram_tensor("maskq", [NQT, 2, 128, 128], F32, kind="ExternalInput").ap()
    ident_in = nc.dram_tensor("ident", [128, 128], F32, kind="ExternalInput").ap()

    out_d = nc.dram_tensor("out", [NQ, D], F32, kind="ExternalOutput").ap()

    xnq_nat = nc.dram_tensor("xnq_nat", [NQ, D], BF16).ap()
    xn_nat = nc.dram_tensor("xn_nat", [S, D], BF16).ap()
    qT_scr = nc.dram_tensor("qT_scr", [H, 128, NQ], BF16).ap()
    kT_scr = nc.dram_tensor("kT_scr", [H, 128, S], BF16).ap()
    v_scr = nc.dram_tensor("v_scr", [H, NKT, 128, 128], BF16).ap()
    s_scr = nc.dram_tensor("s_scr", [NQ, D], BF16).ap()
    xnqT_scr = nc.dram_tensor("xnqT_scr", [128, 32 * NQ], BF16).ap()

    dbg = {}
    if debug:
        for name, shape in [("dbg_xn", [S, D]), ("dbg_xnq", [NQ, D]),
                            ("dbg_qt", [H * 128, NQ]), ("dbg_kt", [H * 128, S]),
                            ("dbg_v", [H * NKT * 128, 128]), ("dbg_s", [NQ, D]),
                            ("dbg_mv", [128, 3 * NTILE_LN])]:
            dbg[name] = nc.dram_tensor(name, shape, F32, kind="ExternalOutput").ap()

    class _StopEmit(Exception):
        pass

    with ExitStack() as ctx:
        e = ctx.enter_context

        sem_names = ["vi", "ti", "aln", "vln",
                     "tqk", "aqk", "vqk", "tvh",
                     "ts4", "tpv", "tz", "tzc", "tt1", "tt2",
                     "vm", "vraw", "vz", "vzi", "vnat", "aexp", "aoev",
                     "two", "vwo", "tff", "vwf", "aff", "vout", "hz"]
        SEM = {n: e(nc.semaphore(n)) for n in sem_names}
        CN = {n: 0 for n in sem_names}

        def inc(instr, name, n=1):
            instr.then_inc(SEM[name], n)
            CN[name] += n
            return instr

        c0 = Ch(nc, e, "c0", 3)
        ctq = Ch(nc, e, "ctq", 2)
        ctk = Ch(nc, e, "ctk", 2)
        cmsk = Ch(nc, e, "cmsk", 1)
        xin_c = Ch(nc, e, "xi", 2)
        xst = Ch(nc, e, "xs", 2)
        wsl = Ch(nc, e, "wl", 2)
        xbr = Ch(nc, e, "xb", 8)
        xbs = Ch(nc, e, "xv", 4)
        qst = Ch(nc, e, "qs", 8)
        athq = Ch(nc, e, "aq", 2)
        athk = Ch(nc, e, "ak", 2)
        athv = Ch(nc, e, "av", 2)
        xqs = Ch(nc, e, "xq", 2)
        sst = Ch(nc, e, "ss", 2)
        sti = Ch(nc, e, "st", 2)
        xti = Ch(nc, e, "xt", 2)
        ost = Ch(nc, e, "os", 2)
        cdb = Ch(nc, e, "db", 1)

        cpool = e(nc.sbuf_tensor("cpool", [128, 2], F32))
        ones_bf = e(nc.sbuf_tensor("ones_bf", [128, 1], BF16))
        ones128 = e(nc.sbuf_tensor("ones128", [128, 128], BF16))
        onesf_row = e(nc.sbuf_tensor("onesf_row", [1, 128], F32))
        ident_bf = e(nc.sbuf_tensor("ident_bf", [128, 128], BF16))
        biasrow_f = e(nc.sbuf_tensor("biasrow_f", [1, D], F32))
        biasrow_bf = e(nc.sbuf_tensor("biasrow_bf", [1, D], BF16))
        ws_sb = e(nc.sbuf_tensor("ws_sb", [1, 1], F32))
        ws_col = e(nc.sbuf_tensor("ws_col", [128, 1], F32))
        wsinv_f = e(nc.sbuf_tensor("wsinv_f", [1, 1], F32))
        wsinv_row_bf = e(nc.sbuf_tensor("wsinv_row_bf", [1, 128], BF16))
        statbuf = e(nc.sbuf_tensor("statbuf", [128, 8], F32))
        mvbuf = e(nc.sbuf_tensor("mvbuf", [128, 3 * NTILE_LN], F32))
        rsq = e(nc.sbuf_tensor("rsq", [128, NQT], F32))
        ucol = e(nc.sbuf_tensor("ucol", [128, NQT], F32))

        block = e(nc.Block())

        def w_slab_ap(w_ap, outer_idx, width):
            base = outer_idx * 32 * 128 * width
            return bass.AP(w_ap.tensor, base,
                           [[width, 128], [128 * width, 32], [1, width]])

        def emit_proj_T(w_r_ap, n_t, n_tb, cos_sb, sin_sb, scr_ap, slab,
                        ps, qraw, rotbuf, t2buf, xsrcT, xbr_chunks):
            t_base, a_base, v_base = CN["tqk"], CN["aqk"], CN["vqk"]
            qst_base = qst.totals()
            slab_v = {}

            def st_val(hc, tb):
                return qst_base[(hc % 2) * n_tb + tb] + 16 * (hc // 2 + 1)

            @block.gpsimd
            def _(g):
                for hcp in range(16):
                    if hcp >= 2:
                        g.wait_ge(SEM["tqk"], t_base + (2 * (hcp - 1)) * n_tb)
                    else:
                        g.wait_ge(SEM["tqk"], t_base)
                    slab_v[hcp] = wsl.dma(g, hcp % 2,
                                          slab[:, (hcp % 2) * 32 * 256:
                                               (hcp % 2 + 1) * 32 * 256],
                                          w_slab_ap(w_r_ap, hcp, 256))

            @block.tensor
            def _(t):
                for hc in range(32):
                    hcp = hc // 2
                    wsl.wait(t, hcp % 2, slab_v[hcp])
                    if hc >= 2:
                        t.wait_ge(SEM["aqk"], a_base + (hc - 1) * n_tb)
                    for ci in range(32):
                        if hc == 0:
                            sem_, val_ = xbr_chunks[ci]
                            t.wait_ge(sem_, val_)
                        lo = (hcp % 2) * 32 * 256 + ci * 256 + (hc % 2) * 128
                        lhsT = slab[:, lo:lo + 128]
                        for tb in range(n_tb):
                            mm = t.matmul(
                                ps[:, ((hc % 2) * n_tb + tb) * 512:
                                   ((hc % 2) * n_tb + tb + 1) * 512],
                                lhsT=lhsT,
                                rhs=xsrcT[:, ci * n_t + tb * 512:ci * n_t + tb * 512 + 512],
                                start=(ci == 0), stop=(ci == 31))
                            if ci == 31:
                                inc(mm, "tqk")

            @block.scalar
            def _(s):
                for hc in range(32):
                    for tb in range(n_tb):
                        s.wait_ge(SEM["tqk"], t_base + hc * n_tb + tb + 1)
                        slot = (hc % 2) * n_tb + tb
                        if hc >= 2:
                            qst.wait(s, slot, st_val(hc - 2, tb))
                        inc(s.copy(out=qraw[:, slot * 512:(slot + 1) * 512],
                                   in_=ps[:, slot * 512:(slot + 1) * 512]), "aqk")

            @block.vector
            def _(v):
                for hc in range(32):
                    for tb in range(n_tb):
                        v.wait_ge(SEM["aqk"], a_base + hc * n_tb + tb + 1)
                        slot = (hc % 2) * n_tb + tb
                        qr = qraw[:, slot * 512:(slot + 1) * 512]
                        c_sl = cos_sb[:, tb * 512:(tb + 1) * 512]
                        s_sl = sin_sb[:, tb * 512:(tb + 1) * 512]
                        inc(v.tensor_scalar_mul(out=rotbuf[0:64, :], in0=qr[64:128, :],
                                                scalar1=-1.0), "hz")
                        inc(v.tensor_copy(out=rotbuf[64:128, :], in_=qr[0:64, :]), "hz")
                        v.wait_ge(SEM["hz"], CN["hz"])
                        inc(v.tensor_mul(out=t2buf[:, :], in0=rotbuf[:, :], in1=s_sl), "hz")
                        inc(v.tensor_mul(out=qr, in0=qr, in1=c_sl), "hz")
                        v.wait_ge(SEM["hz"], CN["hz"])
                        inc(v.tensor_add(out=qr, in0=qr, in1=t2buf[:, :]), "vqk")

            @block.sync
            def _(sy):
                for hc in range(32):
                    for tb in range(n_tb):
                        sy.wait_ge(SEM["vqk"], v_base + hc * n_tb + tb + 1)
                        slot = (hc % 2) * n_tb + tb
                        dst = bass.AP(scr_ap.tensor, (hc * 128) * n_t + tb * 512,
                                      [[n_t, 128], [1, 512]])
                        got = qst.dma(sy, slot, dst,
                                      qraw[:, slot * 512:(slot + 1) * 512])
                        assert got == st_val(hc, tb), (hc, tb, got)


        with nc.sbuf_tensor("xqnT", [128, 32 * NQ], BF16) as xqnT, \
             nc.sbuf_tensor("slab2", [128, 2 * 32 * 256], BF16) as slab2, \
             nc.sbuf_tensor("cosq_sb", [128, NQ], BF16) as cosq_sb, \
             nc.sbuf_tensor("sinq_sb", [128, NQ], BF16) as sinq_sb, \
             nc.sbuf_tensor("qraw2", [128, 4 * 512], BF16) as qraw2, \
             nc.sbuf_tensor("rot2", [128, 512], BF16) as rot2, \
             nc.sbuf_tensor("t2b2", [128, 512], BF16) as t2b2, \
             nc.psum_tensor("ps2", [128, 4 * 512], F32) as ps2:
            with nc.sbuf_tensor("xin", [128, 2 * D], F32) as xin, \
                 nc.sbuf_tensor("xnout", [128, 2 * D], BF16) as xnout, \
                 nc.sbuf_tensor("sqdump", [128, D], F32) as sqdump, \
                 nc.psum_tensor("ps0", [128, 8], F32) as ps0:

                def ln_src(tt):
                    if tt < NQT:
                        return xq_in[tt * 128:(tt + 1) * 128, :]
                    t = tt - NQT
                    return x_in[t * 128:(t + 1) * 128, :]

                def ln_dst(tt):
                    if tt < NQT:
                        return xnq_nat[tt * 128:(tt + 1) * 128, :]
                    t = tt - NQT
                    return xn_nat[t * 128:(t + 1) * 128, :]

                ld = {}
                st_v = {}
                xnq_marks = [None]

                def aln_base(tt):
                    return 3 * min(tt, NQT) + 2 * max(0, tt - NQT)

                @block.sync
                def _(sy):
                    c0.dma(sy, 0, biasrow_f[:, :], bias_in)
                    c0.dma(sy, 1, ws_sb[:, :], ws_in)
                    for tt in range(NTILE_LN):
                        if tt >= 2:
                            sy.wait_ge(SEM["vln"], 2 * (tt - 2) + 2)
                        ld[tt] = xin_c.dma(sy, tt % 2,
                                           xin[:, (tt % 2) * D:(tt % 2 + 1) * D],
                                           ln_src(tt))
                        if tt >= 1:
                            sy.wait_ge(SEM["vln"], 2 * (tt - 1) + 2)
                            st_v[tt - 1] = xst.dma(sy, (tt - 1) % 2, ln_dst(tt - 1),
                                                   xnout[:, ((tt - 1) % 2) * D:
                                                         ((tt - 1) % 2 + 1) * D])
                            if tt - 1 == NQT - 1:
                                xnq_marks[0] = xst.totals()
                    sy.wait_ge(SEM["vln"], 2 * NTILE_LN)
                    st_v[NTILE_LN - 1] = xst.dma(
                        sy, (NTILE_LN - 1) % 2, ln_dst(NTILE_LN - 1),
                        xnout[:, ((NTILE_LN - 1) % 2) * D:((NTILE_LN - 1) % 2 + 1) * D])

                @block.gpsimd
                def _(g):
                    c0.dma(g, 2, ident_bf[:, :], ident_in)

                @block.vector
                def _(v):
                    v.memset(cpool[:, 0:1], -SHIFT)
                    v.memset(cpool[:, 1:2], 1e-5)
                    v.memset(ones_bf[:, :], 1.0)
                    v.memset(ones128[:, :], 1.0)
                    v.memset(mvbuf[:, :], 0.0)
                    inc(v.memset(onesf_row[:, :], 1.0), "vi")
                    c0.wait_all(v)
                    v.tensor_copy(out=biasrow_bf[:, :], in_=biasrow_f[:, :])
                    inc(v.reciprocal(out=wsinv_f[:, :], in_=ws_sb[:, :]), "hz")
                    v.wait_ge(SEM["hz"], CN["hz"])
                    v.tensor_scalar_mul(out=wsinv_row_bf[:, :], in0=onesf_row[:, :],
                                        scalar1=wsinv_f[0:1, 0:1])
                    v.wait_ge(SEM["ti"], 1)
                    inc(v.tensor_copy(out=ws_col[:, :], in_=ps0[:, 0:1]), "vi")
                    for tt in range(NTILE_LN):
                        xin_c.wait(v, tt % 2, ld[tt])
                        if tt >= 1:
                            v.wait_ge(SEM["vln"], 2 * tt)
                        sl = xin[:, (tt % 2) * D:(tt % 2 + 1) * D]
                        inc(v.reduce_sum(out=statbuf[:, 0:1], in_=sl, axis=AX.X), "hz")
                        v.wait_ge(SEM["hz"], CN["hz"])
                        v.wait_ge(SEM["aln"], aln_base(tt) + 1)
                        inc(v.tensor_scalar_mul(out=statbuf[:, 2:3], in0=statbuf[:, 0:1],
                                                scalar1=1.0 / D), "hz")
                        v.wait_ge(SEM["hz"], CN["hz"])
                        inc(v.tensor_mul(out=statbuf[:, 3:4], in0=statbuf[:, 2:3],
                                         in1=statbuf[:, 2:3]), "hz")
                        inc(v.tensor_scalar_mul(out=statbuf[:, 7:8], in0=statbuf[:, 1:2],
                                                scalar1=1.0 / D), "hz")
                        v.wait_ge(SEM["hz"], CN["hz"])
                        inc(v.tensor_sub(out=statbuf[:, 4:5], in0=statbuf[:, 7:8],
                                         in1=statbuf[:, 3:4]), "vln")
                        v.wait_ge(SEM["aln"], aln_base(tt) + 2)
                        inc(v.reciprocal(out=statbuf[:, 6:7], in_=statbuf[:, 5:6]), "hz")
                        v.wait_ge(SEM["hz"], CN["hz"])
                        if tt >= 2:
                            xst.wait(v, tt % 2, st_v[tt - 2])
                            if tt - 2 < NQT:
                                v.wait_ge(SEM["aln"], aln_base(tt - 2) + 3)
                        inc(v.tensor_scalar(out=xnout[:, (tt % 2) * D:(tt % 2 + 1) * D],
                                            in0=sl,
                                            scalar1=statbuf[:, 2:3],
                                            scalar2=statbuf[:, 6:7],
                                            op0=ALU.subtract, op1=ALU.mult), "vln")

                @block.tensor
                def _(t):
                    t.wait_ge(SEM["vi"], 1)
                    c0.wait_all(t)
                    inc(t.matmul(ps0[:, 0:1], lhsT=onesf_row[:, :], rhs=ws_sb[0:1, 0:1],
                                 start=True, stop=True), "ti")

                xbr_q_chunks = []

                @block.scalar
                def _(s):
                    s.wait_ge(SEM["vi"], 1)
                    for tt in range(NTILE_LN):
                        base = aln_base(tt)
                        xin_c.wait(s, tt % 2, ld[tt])
                        if tt >= 1:
                            s.wait_ge(SEM["aln"], base)
                            s.wait_ge(SEM["vln"], 2 * (tt - 1) + 1)
                        sl = xin[:, (tt % 2) * D:(tt % 2 + 1) * D]
                        inc(s.activation(out=sqdump[:, :], in_=sl, func=AF.Square,
                                         accum_out=statbuf[:, 1:2]), "aln")
                        s.wait_ge(SEM["vln"], 2 * tt + 1)
                        inc(s.activation(out=statbuf[:, 5:6], in_=statbuf[:, 4:5],
                                         func=AF.Sqrt, bias=cpool[:, 1:2]), "aln")
                        if tt < NQT:
                            s.wait_ge(SEM["vln"], 2 * tt + 2)
                            inc(s.activation(out=sqdump[:, :],
                                             in_=xnout[:, (tt % 2) * D:(tt % 2 + 1) * D],
                                             func=AF.Copy, accum_out=rsq[:, tt:tt + 1]),
                                "aln")
                        else:
                            for k in range(2):
                                ci = 2 * (tt - NQT) + k
                                if ci == 0:
                                    for i, sem in enumerate(xst.sems):
                                        if xnq_marks[0][i]:
                                            s.wait_ge(sem, xnq_marks[0][i])
                                xbs.wait(s, ci % 4)
                                val = xbs.dma(s, ci % 4,
                                              xqnT[:, ci * NQ:(ci + 1) * NQ],
                                              xnq_nat[0:NQ, ci * 128:(ci + 1) * 128],
                                              transpose=True)
                                xbr_q_chunks.append((xbs.sems[ci % 4], val))

            if upto >= 2:

                @block.gpsimd
                def _(g):
                    ctq.dma(g, 0, cosq_sb[:, :], cosq_in)
                    ctq.dma(g, 1, sinq_sb[:, :], sinq_in)

                xbs_q_tot = xbs.totals()

                @block.vector
                def _(v):
                    ctq.wait_all(v)

                emit_proj_T(wq_r, NQ, 2, cosq_sb, sinq_sb, qT_scr, slab2,
                            ps2, qraw2, rot2, t2b2, xqnT, xbr_q_chunks)

                @block.gpsimd
                def _(g):
                    for i, sem in enumerate(xbs.sems):
                        if xbs_q_tot[i]:
                            g.wait_ge(sem, xbs_q_tot[i])
                    for qt in range(NQT):
                        if qt >= 2:
                            xqs.wait(g, qt % 2)
                        src = bass.AP(xqnT, qt * 128,
                                      [[32 * NQ, 128], [NQ, 32], [1, 128]])
                        xqs.dma(g, qt % 2,
                                xnqT_scr[0:128, qt * 4096:(qt + 1) * 4096], src)
        P2_TQK = CN["tqk"]
        P2_AQK = CN["aqk"]

        try:
            if upto < 2:
                raise _StopEmit
            if upto < 3:
                raise _StopEmit
            with nc.sbuf_tensor("xnT", [128, 32 * S], BF16) as xnT, \
                 nc.sbuf_tensor("cosk_sb", [128, S], BF16) as cosk_sb, \
                 nc.sbuf_tensor("sink_sb", [128, S], BF16) as sink_sb:

                @block.gpsimd
                def _(g):
                    g.wait_ge(SEM["tqk"], P2_TQK)
                    qst.wait_all(g)
                    xst.wait_all(g)
                    xqs.wait_all(g)
                    ctk.dma(g, 0, cosk_sb[:, :], cosk_in)
                    ctk.dma(g, 1, sink_sb[:, :], sink_in)

                xbr_k_chunks = [None] * 32

                @block.sync
                def _(sy):
                    sy.wait_ge(SEM["tqk"], P2_TQK)
                    qst.wait_all(sy)
                    xst.wait_all(sy)
                    xqs.wait_all(sy)
                    for ci in range(32):
                        xbr.wait(sy, ci % 8)
                        val = xbr.dma(sy, ci % 8, xnT[:, ci * S:(ci + 1) * S],
                                      xn_nat[0:S, ci * 128:(ci + 1) * 128],
                                      transpose=True)
                        xbr_k_chunks[ci] = (xbr.sems[ci % 8], val)

                with nc.sbuf_tensor("slab3", [128, 2 * 32 * 256], BF16) as slab3, \
                     nc.sbuf_tensor("qraw3", [128, 8 * 512], BF16) as qraw3, \
                     nc.sbuf_tensor("rot3", [128, 512], BF16) as rot3, \
                     nc.sbuf_tensor("t2b3", [128, 512], BF16) as t2b3, \
                     nc.psum_tensor("ps3", [128, 8 * 512], F32) as ps3:

                    @block.vector
                    def _(v):
                        ctk.wait_all(v)

                    emit_proj_T(wk_r, S, 4, cosk_sb, sink_sb, kT_scr, slab3,
                                ps3, qraw3, rot3, t2b3, xnT, xbr_k_chunks)

                P3K_TQK = CN["tqk"]

                with nc.sbuf_tensor("vslab", [128, 32 * 512], BF16) as vslab, \
                     nc.sbuf_tensor("vev", [128, 8 * 512], BF16) as vev, \
                     nc.psum_tensor("psv", [128, 8 * 512], F32) as psv:

                    tv_base = CN["tqk"]
                    av_base = CN["aqk"]
                    tvh_base = CN["tvh"]
                    vq_base = qst.totals()
                    vslab_v = {}

                    def vst_val(halfidx, tt8):
                        return vq_base[tt8] + 16 * (halfidx + 1)

                    @block.gpsimd
                    def _(g):
                        qst.wait_all(g)
                        for vb in range(8):
                            for hf in range(2):
                                if hf == 0:
                                    if vb >= 1:
                                        g.wait_ge(SEM["tvh"], tvh_base + vb)
                                    else:
                                        g.wait_ge(SEM["tqk"], tv_base)
                                else:
                                    g.wait_ge(SEM["tqk"], tv_base + vb * 16)
                                src = bass.AP(wv_r.tensor,
                                              (vb * 32 + hf * 16) * 128 * 512,
                                              [[512, 128], [128 * 512, 16], [1, 512]])
                                vslab_v[(vb, hf)] = wsl.dma(
                                    g, hf, vslab[:, hf * 16 * 512:(hf + 1) * 16 * 512], src)

                    @block.tensor
                    def _(t):
                        for vb in range(8):
                            for half in range(2):
                                if half == 0:
                                    wsl.wait(t, 0, vslab_v[(vb, 0)])
                                if vb * 2 + half >= 1:
                                    t.wait_ge(SEM["aqk"], av_base + (vb * 2 + half) * 8)
                                for ci in range(32):
                                    if half == 0 and ci == 16:
                                        wsl.wait(t, 1, vslab_v[(vb, 1)])
                                    for tt8 in range(8):
                                        tt = half * 8 + tt8
                                        mm = t.matmul(
                                            psv[:, tt8 * 512:(tt8 + 1) * 512],
                                            lhsT=xnT[:, ci * S + tt * 128:
                                                     ci * S + tt * 128 + 128],
                                            rhs=vslab[:, ci * 512:(ci + 1) * 512],
                                            start=(ci == 0), stop=(ci == 31))
                                        if half == 1 and ci == 15 and tt8 == 7:
                                            inc(mm, "tvh")
                                        if ci == 31:
                                            inc(mm, "tqk")

                    @block.scalar
                    def _(s):
                        cnt = 0
                        for vb in range(8):
                            for half in range(2):
                                for tt8 in range(8):
                                    cnt += 1
                                    s.wait_ge(SEM["tqk"], tv_base + cnt)
                                    if vb * 2 + half >= 1:
                                        qst.wait(s, tt8, vst_val(vb * 2 + half - 1, tt8))
                                    inc(s.copy(out=vev[:, tt8 * 512:(tt8 + 1) * 512],
                                               in_=psv[:, tt8 * 512:(tt8 + 1) * 512]), "aqk")

                    @block.sync
                    def _(sy):
                        cnt = 0
                        for vb in range(8):
                            for half in range(2):
                                for tt8 in range(8):
                                    tt = half * 8 + tt8
                                    cnt += 1
                                    sy.wait_ge(SEM["aqk"], av_base + cnt)
                                    h0 = vb * 4
                                    dst = bass.AP(v_scr.tensor,
                                                  (h0 * NKT + tt) * 128 * 128,
                                                  [[128, 128], [NKT * 128 * 128, 4],
                                                   [1, 128]])
                                    got = qst.dma(
                                        sy, tt8, dst, vev[:, tt8 * 512:(tt8 + 1) * 512])
                                    assert got == vst_val(vb * 2 + half, tt8)
            P3_AQK = CN["aqk"]

            if upto < 4:
                raise _StopEmit
            with nc.sbuf_tensor("outhT", [128, H * NQ], BF16) as outhT:

                with nc.sbuf_tensor("mask_sb", [128, 16 * 128], BF16) as mask_sb, \
                     nc.sbuf_tensor("qh", [128, 2 * NQ], BF16) as qh, \
                     nc.sbuf_tensor("kh", [128, 2 * S], BF16) as kh, \
                     nc.sbuf_tensor("vh", [128, 2 * S], BF16) as vh, \
                     nc.sbuf_tensor("expT", [128, 2 * NQ], BF16) as expT, \
                     nc.sbuf_tensor("zinv_bc", [128, NQ], F32) as zinv_bc, \
                     nc.sbuf_tensor("pv_sb", [128, 2 * NQ], F32) as pv_sb, \
                     nc.sbuf_tensor("z_sb", [128, 2 * NQ], F32) as z_sb, \
                     nc.psum_tensor("psA", [128, 8 * 512], F32) as psA:
                    qv = {}
                    kv = {}
                    vv = {}

                    @block.gpsimd
                    def _(g):
                        g.wait_ge(SEM["aqk"], P3_AQK)
                        qst.wait_all(g)
                        msk = bass.AP(maskq_in.tensor, 0,
                                      [[128, 128], [128 * 128, 16], [1, 128]])
                        cmsk.dma(g, 0, mask_sb[:, :], msk)

                    @block.sync
                    def _(sy):
                        sy.wait_ge(SEM["aqk"], P3_AQK)
                        qst.wait_all(sy)
                        for h in range(H):
                            if h >= 2:
                                sy.wait_ge(SEM["tpv"], (h - 1) * NKT)
                            qv[h] = athq.dma(sy, h % 2,
                                             qh[:, (h % 2) * NQ:(h % 2 + 1) * NQ], qT_scr[h])
                            kv[h] = athk.dma(sy, h % 2,
                                             kh[:, (h % 2) * S:(h % 2 + 1) * S], kT_scr[h])
                            vsrc = bass.AP(v_scr.tensor, h * NKT * 128 * 128,
                                           [[128, 128], [128 * 128, NKT], [1, 128]])
                            vv[h] = athv.dma(sy, h % 2,
                                             vh[:, (h % 2) * S:(h % 2 + 1) * S], vsrc)

                    def strip_off(kt):
                        return (kt % 2) * NQ

                    def seg_list(kt):
                        lo = (kt // 2) * 128
                        out = []
                        for seg in range(0, NQ, 512):
                            a, b = max(seg, lo), seg + 512
                            if b > a:
                                out.append((a, b, seg))
                        return out

                    def last_kt_for_seg(seg):
                        last = -1
                        for kt in range(NKT):
                            if (kt // 2) * 128 < seg + 512:
                                last = kt
                        return last

                    LAST_KT = {seg: last_kt_for_seg(seg) for seg in range(0, NQ, 512)}

                    @block.tensor
                    def _(t):
                        for h in range(H):
                            athq.wait(t, h % 2, qv[h])
                            athk.wait(t, h % 2, kv[h])
                            athv.wait(t, h % 2, vv[h])
                            if h >= 1:
                                t.wait_ge(SEM["aexp"], h * NKT)
                                t.wait_ge(SEM["aoev"], 2 * h)
                            qsl = qh[:, (h % 2) * NQ:(h % 2 + 1) * NQ]
                            ksl = kh[:, (h % 2) * S:(h % 2 + 1) * S]
                            vsl = vh[:, (h % 2) * S:(h % 2 + 1) * S]

                            def scores(kt):
                                lo = (kt // 2) * 128
                                last = None
                                for sl0 in range(0, NQ - lo, 512):
                                    sl1 = min(sl0 + 512, NQ - lo)
                                    last = t.matmul(
                                        psA[:, strip_off(kt) + sl0:strip_off(kt) + sl1],
                                        lhsT=ksl[:, kt * 128:(kt + 1) * 128],
                                        rhs=qsl[:, lo + sl0:lo + sl1],
                                        start=True, stop=True)
                                inc(last, "ts4")

                            def zpv(kt):
                                lo = (kt // 2) * 128
                                mm = None
                                for (a, b, seg) in seg_list(kt):
                                    mm = t.matmul(
                                        psA[:, 3072 + a:3072 + b],
                                        lhsT=ones128[:, :],
                                        rhs=expT[:, strip_off(kt) + a - lo:
                                                 strip_off(kt) + b - lo],
                                        start=(kt == 0), stop=(kt == LAST_KT[seg]))
                                inc(mm, "tz")
                                for (a, b, seg) in seg_list(kt):
                                    mm = t.matmul(
                                        psA[:, 2048 + a:2048 + b],
                                        lhsT=vsl[:, kt * 128:(kt + 1) * 128],
                                        rhs=expT[:, strip_off(kt) + a - lo:
                                                 strip_off(kt) + b - lo],
                                        start=(kt == 0), stop=(kt == LAST_KT[seg]))
                                inc(mm, "tpv")

                            scores(0)
                            scores(1)
                            for kt in range(NKT):
                                t.wait_ge(SEM["aexp"], h * NKT + kt + 1)
                                zpv(kt)
                                if kt + 2 < NKT:
                                    scores(kt + 2)


                    @block.vector
                    def _(v):
                        cmsk.wait(v, 0)

                        def post_step(hp, j):
                            if hp < 0:
                                return
                            if j == 0:
                                v.wait_ge(SEM["aoev"], 2 * hp + 2)
                                if hp >= 1:
                                    v.wait_ge(SEM["vnat"], hp)
                            if j < NQT:
                                inc(v.reciprocal(
                                    out=zinv_bc[:, j * 128:(j + 1) * 128],
                                    in_=z_sb[:, (hp % 2) * NQ + j * 128:
                                             (hp % 2) * NQ + (j + 1) * 128]), "hz")
                            else:
                                v.wait_ge(SEM["hz"], CN["hz"])
                                inc(v.tensor_mul(out=outhT[:, hp * NQ:(hp + 1) * NQ],
                                                 in0=pv_sb[:, (hp % 2) * NQ:
                                                           (hp % 2 + 1) * NQ],
                                                 in1=zinv_bc[:, :]), "vnat")

                        for h in range(H):
                            for kt in range(NKT):
                                j0 = kt // 2
                                v.wait_ge(SEM["ts4"], h * NKT + kt + 1)
                                m_sl = mask_sb[:, (j0 * 2 + (kt % 2)) * 128:
                                               (j0 * 2 + (kt % 2) + 1) * 128]
                                inc(v.tensor_add(
                                    out=psA[:, strip_off(kt):strip_off(kt) + 128],
                                    in0=psA[:, strip_off(kt):strip_off(kt) + 128],
                                    in1=m_sl), "vm")
                                if kt % 2 == 1:
                                    post_step(h - 1, kt // 2)
                            post_step(h - 1, NQT)
                        for j in range(NQT + 1):
                            post_step(H - 1, j)

                    @block.scalar
                    def _(s):
                        for h in range(H):
                            for kt in range(NKT):
                                w = NQ - (kt // 2) * 128
                                s.wait_ge(SEM["vm"], h * NKT + kt + 1)
                                if kt >= 2:
                                    s.wait_ge(SEM["tpv"], h * NKT + kt - 1)
                                    s.wait_ge(SEM["tz"], h * NKT + kt - 1)
                                inc(s.activation(out=expT[:, strip_off(kt):strip_off(kt) + w],
                                                 in_=psA[:, strip_off(kt):strip_off(kt) + w],
                                                 func=AF.Exp, bias=cpool[:, 0:1],
                                                 scale=SM_SCALE), "aexp")
                            s.wait_ge(SEM["tpv"], (h + 1) * NKT)
                            if h >= 2:
                                s.wait_ge(SEM["vnat"], h - 1)
                            inc(s.copy(out=pv_sb[:, (h % 2) * NQ:(h % 2 + 1) * NQ],
                                       in_=psA[:, 2048:3072]), "aoev")
                            s.wait_ge(SEM["tz"], (h + 1) * NKT)
                            inc(s.copy(out=z_sb[:, (h % 2) * NQ:(h % 2 + 1) * NQ],
                                       in_=psA[:, 3072:4096]), "aoev")


                if upto >= 5:
                    with nc.sbuf_tensor("woslab", [128, 2 * 32 * 512], BF16) as woslab, \
                         nc.sbuf_tensor("sbf", [128, 2 * 512], BF16) as sbf, \
                         nc.psum_tensor("ps5", [128, 2 * 512], F32) as ps5:

                        two0 = CN["two"]
                        vwo0 = CN["vwo"]
                        sst_base = sst.totals()
                        woslab_v = {}

                        def sst_val(idx):
                            return sst_base[idx % 2] + 16 * (idx // 2 + 1)

                        @block.gpsimd
                        def _(g):
                            g.wait_ge(SEM["vnat"], H)
                            for ob in range(8):
                                if ob >= 2:
                                    g.wait_ge(SEM["two"], two0 + (ob - 1) * 8)
                                woslab_v[ob] = wsl.dma(
                                    g, ob % 2,
                                    woslab[:, (ob % 2) * 32 * 512:(ob % 2 + 1) * 32 * 512],
                                    w_slab_ap(wo_r, ob, 512))

                        @block.tensor
                        def _(t):
                            t.wait_ge(SEM["vnat"], H)
                            for ob in range(8):
                                wsl.wait(t, ob % 2, woslab_v[ob])
                                for qt in range(NQT):
                                    idx = ob * 8 + qt
                                    if idx >= 2:
                                        t.wait_ge(SEM["vwo"], vwo0 + idx - 1)
                                    for ci in range(32):
                                        mm = t.matmul(
                                            ps5[:, (idx % 2) * 512:(idx % 2 + 1) * 512],
                                            lhsT=outhT[:, ci * NQ + qt * 128:
                                                       ci * NQ + qt * 128 + 128],
                                            rhs=woslab[:, (ob % 2) * 32 * 512 + ci * 512:
                                                       (ob % 2) * 32 * 512 + (ci + 1) * 512],
                                            start=(ci == 0), stop=(ci == 31))
                                    inc(mm, "two")

                        @block.vector
                        def _(v):
                            for idx in range(64):
                                v.wait_ge(SEM["two"], two0 + idx + 1)
                                if idx >= 2:
                                    sst.wait(v, idx % 2, sst_val(idx - 2))
                                inc(v.tensor_copy(
                                    out=sbf[:, (idx % 2) * 512:(idx % 2 + 1) * 512],
                                    in_=ps5[:, (idx % 2) * 512:(idx % 2 + 1) * 512]), "vwo")

                        @block.sync
                        def _(sy):
                            for ob in range(8):
                                for qt in range(NQT):
                                    idx = ob * 8 + qt
                                    sy.wait_ge(SEM["vwo"], vwo0 + idx + 1)
                                    got = sst.dma(
                                        sy, idx % 2,
                                        s_scr[qt * 128:(qt + 1) * 128, ob * 512:(ob + 1) * 512],
                                        sbf[:, (idx % 2) * 512:(idx % 2 + 1) * 512])
                                    assert got == sst_val(idx)

            P5_TWO = CN["two"]
            P5_VWO = CN["vwo"]

            if upto < 6:
                raise _StopEmit
            with nc.sbuf_tensor("xnqT6", [128, 32 * NQ], BF16) as xnqT6, \
                 nc.sbuf_tensor("wft", [128, 2 * 32 * 512], BF16) as wft, \
                 nc.sbuf_tensor("gbuf", [128, 2 * 512], BF16) as gbuf, \
                 nc.sbuf_tensor("stile", [128, 2 * 512], BF16) as stile, \
                 nc.sbuf_tensor("xtile", [128, 2 * 512], F32) as xtile, \
                 nc.sbuf_tensor("sgbuf", [128, 512], F32) as sgbuf, \
                 nc.sbuf_tensor("obuf", [128, 2 * 512], F32) as obuf, \
                 nc.psum_tensor("ps6", [128, 2 * 512], F32) as ps6:

                vwf0 = CN["vwf"]
                ost_base = ost.totals()
                tff0 = CN["tff"]
                aff0 = CN["aff"]
                vout0 = CN["vout"]
                sti_v = {}
                xti_v = {}
                ost_v = {}
                wft_v = {}
                xq6_marks = {}

                @block.gpsimd
                def _(g):
                    g.wait_ge(SEM["two"], P5_TWO)
                    g.wait_ge(SEM["vwo"], P5_VWO)
                    sst.wait_all(g)
                    for ob in range(8):
                        if ob >= 2:
                            g.wait_ge(SEM["tff"], tff0 + (ob - 1) * 8)
                        wft_v[ob] = wsl.dma(
                            g, ob % 2,
                            wft[:, (ob % 2) * 32 * 512:(ob % 2 + 1) * 32 * 512],
                            w_slab_ap(wfT_r, ob, 512))

                @block.sync
                def _(sy):
                    sy.wait_ge(SEM["two"], P5_TWO)
                    sy.wait_ge(SEM["vwo"], P5_VWO)
                    sst.wait_all(sy)
                    for qt in range(NQT):
                        xbr.wait(sy, qt % 8)
                        xbr.dma(sy, qt % 8,
                                xnqT6[:, qt * 4096:(qt + 1) * 4096],
                                xnqT_scr[0:128, qt * 4096:(qt + 1) * 4096])
                        xq6_marks[qt] = xbr.totals()
                    for ob_s in range(8):
                        for qt in range(NQT):
                            idx = ob_s * 8 + qt
                            sy.wait_ge(SEM["vout"], vout0 + idx + 1)
                            ost_v[idx] = ost.dma(
                                sy, idx % 2,
                                out_d[qt * 128:(qt + 1) * 128,
                                      ob_s * 512:(ob_s + 1) * 512],
                                obuf[:, (idx % 2) * 512:(idx % 2 + 1) * 512])
                    ost.wait_all(sy)

                @block.scalar
                def _(s):
                    s.wait_ge(SEM["two"], P5_TWO)
                    s.wait_ge(SEM["vwo"], P5_VWO)
                    sst.wait_all(s)
                    for ob in range(8):
                        for qt in range(NQT):
                            idx = ob * 8 + qt
                            if idx >= 2:
                                s.wait_ge(SEM["vout"], vout0 + idx - 1)
                            sti_v[idx] = sti.dma(
                                s, idx % 2,
                                stile[:, (idx % 2) * 512:(idx % 2 + 1) * 512],
                                s_scr[qt * 128:(qt + 1) * 128, ob * 512:(ob + 1) * 512])
                            xti_v[idx] = xti.dma(
                                s, idx % 2,
                                xtile[:, (idx % 2) * 512:(idx % 2 + 1) * 512],
                                xq_in[qt * 128:(qt + 1) * 128, ob * 512:(ob + 1) * 512])
                            s.wait_ge(SEM["tff"], tff0 + idx + 1)
                            if idx == 0:
                                s.wait_ge(SEM["vwf"], vwf0 + 1)
                            inc(s.activation(out=gbuf[:, (idx % 2) * 512:(idx % 2 + 1) * 512],
                                             in_=ps6[:, (idx % 2) * 512:(idx % 2 + 1) * 512],
                                             func=AF.Silu, scale=ws_col[:, 0:1],
                                             bias=ucol[:, qt:qt + 1]), "aff")

                @block.vector
                def _(v):
                    v.wait_ge(SEM["vi"], 2)
                    v.wait_ge(SEM["aln"], 3 * NQT)
                    inc(v.tensor_scalar_mul(out=ucol[:, :], in0=rsq[:, :],
                                            scalar1=ws_col[:, 0:1]), "hz")
                    v.wait_ge(SEM["hz"], CN["hz"])
                    inc(v.tensor_scalar_mul(out=ucol[:, :], in0=ucol[:, :],
                                            scalar1=-1.5), "vwf")
                    for ob in range(8):
                        for qt in range(NQT):
                            idx = ob * 8 + qt
                            v.wait_ge(SEM["aff"], aff0 + idx + 1)
                            sti.wait(v, idx % 2, sti_v[idx])
                            xti.wait(v, idx % 2, xti_v[idx])
                            if idx >= 2:
                                ost.wait(v, idx % 2,
                                         ost_base[idx % 2] + 16 * ((idx - 2) // 2 + 1))
                            if idx >= 1:
                                v.wait_ge(SEM["vout"], vout0 + idx)
                            inc(v.tensor_mul(out=sgbuf[:, :],
                                             in0=gbuf[:, (idx % 2) * 512:(idx % 2 + 1) * 512],
                                             in1=stile[:, (idx % 2) * 512:(idx % 2 + 1) * 512]),
                                "hz")
                            v.wait_ge(SEM["hz"], CN["hz"])
                            inc(v.tensor_add(
                                out=obuf[:, (idx % 2) * 512:(idx % 2 + 1) * 512],
                                in0=sgbuf[:, :],
                                in1=xtile[:, (idx % 2) * 512:(idx % 2 + 1) * 512]), "vout")

                @block.tensor
                def _(t):
                    t.wait_ge(SEM["vwo"], P5_VWO)
                    for ob in range(8):
                        wsl.wait(t, ob % 2, wft_v[ob])
                        for qt in range(NQT):
                            idx = ob * 8 + qt
                            if ob == 0:
                                for i, sem in enumerate(xbr.sems):
                                    t.wait_ge(sem, xq6_marks[qt][i])
                            if idx >= 2:
                                t.wait_ge(SEM["aff"], aff0 + idx - 1)
                            for ci in range(32):
                                t.matmul(ps6[:, (idx % 2) * 512:(idx % 2 + 1) * 512],
                                         lhsT=xnqT6[:, qt * 4096 + ci * 128:
                                                    qt * 4096 + (ci + 1) * 128],
                                         rhs=wft[:, (ob % 2) * 32 * 512 + ci * 512:
                                                 (ob % 2) * 32 * 512 + (ci + 1) * 512],
                                         start=(ci == 0), stop=False)
                            mm = t.matmul(ps6[:, (idx % 2) * 512:(idx % 2 + 1) * 512],
                                          lhsT=wsinv_row_bf[0:1, 0:128],
                                          rhs=biasrow_bf[0:1, ob * 512:(ob + 1) * 512],
                                          start=False, stop=True)
                            inc(mm, "tff")


        except _StopEmit:
            pass

        if debug:
            @block.gpsimd
            def _(g):
                for ch in [c0, ctq, ctk, cmsk, xin_c, xst, wsl, xbr, xbs, qst,
                           athq, athk, athv, xqs, sst, sti, xti, ost]:
                    ch.wait_all(g)
                for nm in ["vln", "vqk", "aqk", "aoev", "vwo", "vout"]:
                    if CN[nm]:
                        g.wait_ge(SEM[nm], CN[nm])
                taps = [(dbg["dbg_mv"], mvbuf[:, :]),
                        (dbg["dbg_xn"], xn_nat[:, :]),
                        (dbg["dbg_xnq"], xnq_nat[:, :])]
                if upto >= 2:
                    taps.append((dbg["dbg_qt"],
                                 bass.AP(qT_scr.tensor, 0, [[NQ, H * 128], [1, NQ]])))
                if upto >= 3:
                    taps.append((dbg["dbg_kt"],
                                 bass.AP(kT_scr.tensor, 0, [[S, H * 128], [1, S]])))
                    taps.append((dbg["dbg_v"],
                                 bass.AP(v_scr.tensor, 0,
                                         [[128, H * NKT * 128], [1, 128]])))
                if upto >= 5:
                    taps.append((dbg["dbg_s"], s_scr[:, :]))
                for dst, src in taps:
                    if cdb.cnt[0]:
                        cdb.wait(g, 0)
                    cdb.dma(g, 0, dst, src)
                cdb.wait_all(g)

    return nc



def _host_tables():
    inv = 1.0 / (10000.0 ** (np.arange(0, HD, 2, dtype=np.float64) / HD))
    pos = np.arange(S, dtype=np.float64)
    fr = np.outer(inv, pos)
    cosT = np.cos(np.concatenate([fr, fr], axis=0)).astype(np.float32)
    sinT = np.sin(np.concatenate([fr, fr], axis=0)).astype(np.float32)
    return cosT, sinT


def _relayout(w, n_outer, width):
    return np.ascontiguousarray(
        w.reshape(32, 128, n_outer, width).transpose(2, 0, 1, 3))


def _masks(qts):
    m = np.zeros((NQT, 2, 128, 128), dtype=np.float32)
    p = np.arange(128)
    ecols = np.arange(128)
    for j, qt in enumerate(qts):
        for i in range(2):
            kt = 2 * j + i
            valid = (kt * 128 + p[:, None]) <= (qt * 128 + ecols[None, :])
            m[j, i] = np.where(valid, 0.0, -1e30)
    return m


def kernel(L0_attn_hidden_states, L0_attn_wq, L0_attn_wk, L0_attn_wv, L0_attn_wo,
           L0_attn_kq1, L0_attn_kq2, L0_attn_kk1, L0_attn_kk2, L0_attn_scalingD,
           L0_attn_interaction_k, L0_attn_scalingD2, L0_ffn_weight_2bit,
           L0_ffn_weight_scale, L0_ffn_bias):
    x = np.ascontiguousarray(np.asarray(L0_attn_hidden_states, dtype=np.float32))
    wq = np.asarray(L0_attn_wq, dtype=np.float32)
    wk = np.asarray(L0_attn_wk, dtype=np.float32)
    wv = np.asarray(L0_attn_wv, dtype=np.float32)
    wo = np.asarray(L0_attn_wo, dtype=np.float32)
    wf = np.asarray(L0_ffn_weight_2bit, dtype=np.int32)
    ws = np.ascontiguousarray(np.asarray(L0_ffn_weight_scale, dtype=np.float32).reshape(1, 1))
    bias = np.ascontiguousarray(np.asarray(L0_ffn_bias, dtype=np.float32).reshape(1, D))

    cosT, sinT = _host_tables()
    ident = np.eye(128, dtype=np.float32)
    wq_r = _relayout(wq, 16, 256)
    wk_r = _relayout(wk, 16, 256)
    wv_r = _relayout(wv, 8, 512)
    wo_r = _relayout(wo, 8, 512)
    wfT_r = _relayout(wf.T, 8, 512)
    masks = [_masks(QT_SETS[0]), _masks(QT_SETS[1])]
    rows_sets = [np.concatenate([np.arange(qt * 128, (qt + 1) * 128) for qt in qs])
                 for qs in QT_SETS]

    in_maps = []
    for c in range(8):
        beta, par = c // 2, c % 2
        rows = rows_sets[par]
        in_maps.append({
            "x": np.ascontiguousarray(x[beta]),
            "xq": np.ascontiguousarray(x[beta][rows]),
            "wq_r": wq_r, "wk_r": wk_r, "wv_r": wv_r, "wo_r": wo_r,
            "wfT_r": wfT_r, "bias": bias, "ws": ws,
            "cosq": np.ascontiguousarray(cosT[:, rows]),
            "sinq": np.ascontiguousarray(sinT[:, rows]),
            "cosk": cosT, "sink": sinT,
            "maskq": masks[par], "ident": ident,
        })

    if "nc" not in _NC_CACHE:
        _NC_CACHE["nc"] = build(False)
    nc = _NC_CACHE["nc"]
    res = run_bass_kernel_spmd(nc, in_maps, core_ids=list(range(8)))

    out = np.empty((4, S, D), dtype=np.float32)
    for c in range(8):
        beta, par = c // 2, c % 2
        out[beta][rows_sets[par]] = np.asarray(res.results[c]["out"], dtype=np.float32)
    return out


# version 13
# speedup vs baseline: 1.0382x; 1.0029x over previous
import contextlib
import math
import numpy as np
import ml_dtypes
import concourse.bass as bass
import concourse.mybir as mybir
from concourse.bass_utils import run_bass_kernel_spmd
from contextlib import ExitStack

F32 = mybir.dt.float32
BF16 = mybir.dt.bfloat16
I32 = mybir.dt.int32
AF = mybir.ActivationFunctionType
ALU = mybir.AluOpType
AX = mybir.AxisListType

S, D, H, HD = 2048, 4096, 32, 128
NQT = 8
NQ = NQT * 128
NKT = 16
QT_SETS = ([0, 3, 4, 7, 8, 11, 12, 15], [1, 2, 5, 6, 9, 10, 13, 14])
SHIFT = 18.0
SM_SCALE = 1.0 / math.sqrt(HD)
NTILE_LN = NQT + S // 128

_NC_CACHE = {}


class Ch:

    def __init__(self, nc, e, name, nslots):
        self.sems = [e(nc.semaphore(f"{name}{i}")) for i in range(nslots)]
        self.cnt = [0] * nslots

    def dma(self, eng, slot, out, in_, **kw):
        eng.dma_start(out=out, in_=in_, **kw).then_inc(self.sems[slot], 16)
        self.cnt[slot] += 16
        return self.cnt[slot]

    def wait(self, eng, slot, val=None):
        eng.wait_ge(self.sems[slot], self.cnt[slot] if val is None else val)

    def wait_all(self, eng):
        for i, s in enumerate(self.sems):
            if self.cnt[i]:
                eng.wait_ge(s, self.cnt[i])

    def totals(self):
        return list(self.cnt)


def build(debug=False, upto=6):
    nc = bass.Bass("TRN2", target_bir_lowering=False)

    x_in = nc.dram_tensor("x", [S, D], F32, kind="ExternalInput").ap()
    xq_in = nc.dram_tensor("xq", [NQ, D], F32, kind="ExternalInput").ap()
    wq_r = nc.dram_tensor("wq_r", [16, 32, 128, 256], F32, kind="ExternalInput").ap()
    wk_r = nc.dram_tensor("wk_r", [16, 32, 128, 256], F32, kind="ExternalInput").ap()
    wv_r = nc.dram_tensor("wv_r", [8, 32, 128, 512], F32, kind="ExternalInput").ap()
    wo_r = nc.dram_tensor("wo_r", [8, 32, 128, 512], F32, kind="ExternalInput").ap()
    wfT_r = nc.dram_tensor("wfT_r", [8, 32, 128, 512], I32, kind="ExternalInput").ap()
    bias_in = nc.dram_tensor("bias", [1, D], F32, kind="ExternalInput").ap()
    ws_in = nc.dram_tensor("ws", [1, 1], F32, kind="ExternalInput").ap()
    cosq_in = nc.dram_tensor("cosq", [128, NQ], F32, kind="ExternalInput").ap()
    sinq_in = nc.dram_tensor("sinq", [128, NQ], F32, kind="ExternalInput").ap()
    cosk_in = nc.dram_tensor("cosk", [128, S], F32, kind="ExternalInput").ap()
    sink_in = nc.dram_tensor("sink", [128, S], F32, kind="ExternalInput").ap()
    maskq_in = nc.dram_tensor("maskq", [NQT, 2, 128, 128], F32, kind="ExternalInput").ap()
    ident_in = nc.dram_tensor("ident", [128, 128], F32, kind="ExternalInput").ap()

    out_d = nc.dram_tensor("out", [NQ, D], F32, kind="ExternalOutput").ap()

    xnq_nat = nc.dram_tensor("xnq_nat", [NQ, D], BF16).ap()
    xn_nat = nc.dram_tensor("xn_nat", [S, D], BF16).ap()
    qT_scr = nc.dram_tensor("qT_scr", [H, 128, NQ], BF16).ap()
    kT_scr = nc.dram_tensor("kT_scr", [H, 128, S], BF16).ap()
    v_scr = nc.dram_tensor("v_scr", [H, NKT, 128, 128], BF16).ap()
    s_scr = nc.dram_tensor("s_scr", [NQ, D], BF16).ap()
    xnqT_scr = nc.dram_tensor("xnqT_scr", [128, 32 * NQ], BF16).ap()

    dbg = {}
    if debug:
        for name, shape in [("dbg_xn", [S, D]), ("dbg_xnq", [NQ, D]),
                            ("dbg_qt", [H * 128, NQ]), ("dbg_kt", [H * 128, S]),
                            ("dbg_v", [H * NKT * 128, 128]), ("dbg_s", [NQ, D]),
                            ("dbg_mv", [128, 3 * NTILE_LN])]:
            dbg[name] = nc.dram_tensor(name, shape, F32, kind="ExternalOutput").ap()

    class _StopEmit(Exception):
        pass

    with ExitStack() as ctx:
        e = ctx.enter_context

        sem_names = ["vi", "ti", "aln", "vln",
                     "tqk", "aqk", "vqk", "tvh",
                     "ts4", "tpv", "tz", "tzc", "tt1", "tt2",
                     "vm", "vraw", "vz", "vzi", "vnat", "aexp", "aoev",
                     "two", "vwo", "tff", "vwf", "aff", "vout", "hz"]
        SEM = {n: e(nc.semaphore(n)) for n in sem_names}
        CN = {n: 0 for n in sem_names}

        def inc(instr, name, n=1):
            instr.then_inc(SEM[name], n)
            CN[name] += n
            return instr

        c0 = Ch(nc, e, "c0", 3)
        ctq = Ch(nc, e, "ctq", 2)
        ctk = Ch(nc, e, "ctk", 2)
        cmsk = Ch(nc, e, "cmsk", 1)
        xin_c = Ch(nc, e, "xi", 2)
        xst = Ch(nc, e, "xs", 2)
        wsl = Ch(nc, e, "wl", 2)
        xbr = Ch(nc, e, "xb", 8)
        xbs = Ch(nc, e, "xv", 4)
        qst = Ch(nc, e, "qs", 8)
        athq = Ch(nc, e, "aq", 2)
        athk = Ch(nc, e, "ak", 2)
        athv = Ch(nc, e, "av", 2)
        xqs = Ch(nc, e, "xq", 2)
        sst = Ch(nc, e, "ss", 2)
        sti = Ch(nc, e, "st", 2)
        xti = Ch(nc, e, "xt", 2)
        ost = Ch(nc, e, "os", 2)
        cdb = Ch(nc, e, "db", 1)

        cpool = e(nc.sbuf_tensor("cpool", [128, 2], F32))
        ones_bf = e(nc.sbuf_tensor("ones_bf", [128, 1], BF16))
        ones128 = e(nc.sbuf_tensor("ones128", [128, 128], BF16))
        onesf_row = e(nc.sbuf_tensor("onesf_row", [1, 128], F32))
        ident_bf = e(nc.sbuf_tensor("ident_bf", [128, 128], BF16))
        biasrow_f = e(nc.sbuf_tensor("biasrow_f", [1, D], F32))
        biasrow_bf = e(nc.sbuf_tensor("biasrow_bf", [1, D], BF16))
        ws_sb = e(nc.sbuf_tensor("ws_sb", [1, 1], F32))
        ws_col = e(nc.sbuf_tensor("ws_col", [128, 1], F32))
        wsinv_f = e(nc.sbuf_tensor("wsinv_f", [1, 1], F32))
        wsinv_row_bf = e(nc.sbuf_tensor("wsinv_row_bf", [1, 128], BF16))
        statbuf = e(nc.sbuf_tensor("statbuf", [128, 8], F32))
        mvbuf = e(nc.sbuf_tensor("mvbuf", [128, 3 * NTILE_LN], F32))
        rsq = e(nc.sbuf_tensor("rsq", [128, NQT], F32))
        ucol = e(nc.sbuf_tensor("ucol", [128, NQT], F32))

        block = e(nc.Block())

        def w_slab_ap(w_ap, outer_idx, width):
            base = outer_idx * 32 * 128 * width
            return bass.AP(w_ap.tensor, base,
                           [[width, 128], [128 * width, 32], [1, width]])

        def emit_proj_T(w_r_ap, n_t, n_tb, cos_sb, sin_sb, scr_ap, slab,
                        ps, qraw, rotbuf, t2buf, xsrcT, xbr_chunks):
            t_base, a_base, v_base = CN["tqk"], CN["aqk"], CN["vqk"]
            qst_base = qst.totals()
            slab_v = {}

            def st_val(hc, tb):
                return qst_base[(hc % 2) * n_tb + tb] + 16 * (hc // 2 + 1)

            @block.gpsimd
            def _(g):
                for hcp in range(16):
                    if hcp >= 2:
                        g.wait_ge(SEM["tqk"], t_base + (2 * (hcp - 1)) * n_tb)
                    else:
                        g.wait_ge(SEM["tqk"], t_base)
                    slab_v[hcp] = wsl.dma(g, hcp % 2,
                                          slab[:, (hcp % 2) * 32 * 256:
                                               (hcp % 2 + 1) * 32 * 256],
                                          w_slab_ap(w_r_ap, hcp, 256))

            @block.tensor
            def _(t):
                for hc in range(32):
                    hcp = hc // 2
                    wsl.wait(t, hcp % 2, slab_v[hcp])
                    if hc >= 2:
                        t.wait_ge(SEM["aqk"], a_base + (hc - 1) * n_tb)
                    if hc == 0 and isinstance(xbr_chunks, dict):
                        for tb in range(n_tb):
                            for ci in range(32):
                                sem_, val_ = xbr_chunks[(ci, tb)]
                                t.wait_ge(sem_, val_)
                                lo = (hcp % 2) * 32 * 256 + ci * 256 + (hc % 2) * 128
                                mm = t.matmul(
                                    ps[:, ((hc % 2) * n_tb + tb) * 512:
                                       ((hc % 2) * n_tb + tb + 1) * 512],
                                    lhsT=slab[:, lo:lo + 128],
                                    rhs=xsrcT[:, ci * n_t + tb * 512:
                                              ci * n_t + tb * 512 + 512],
                                    start=(ci == 0), stop=(ci == 31))
                                if ci == 31:
                                    inc(mm, "tqk")
                        continue
                    for ci in range(32):
                        if hc == 0:
                            sem_, val_ = xbr_chunks[ci]
                            t.wait_ge(sem_, val_)
                        lo = (hcp % 2) * 32 * 256 + ci * 256 + (hc % 2) * 128
                        lhsT = slab[:, lo:lo + 128]
                        for tb in range(n_tb):
                            mm = t.matmul(
                                ps[:, ((hc % 2) * n_tb + tb) * 512:
                                   ((hc % 2) * n_tb + tb + 1) * 512],
                                lhsT=lhsT,
                                rhs=xsrcT[:, ci * n_t + tb * 512:ci * n_t + tb * 512 + 512],
                                start=(ci == 0), stop=(ci == 31))
                            if ci == 31:
                                inc(mm, "tqk")

            @block.scalar
            def _(s):
                for hc in range(32):
                    for tb in range(n_tb):
                        s.wait_ge(SEM["tqk"], t_base + hc * n_tb + tb + 1)
                        slot = (hc % 2) * n_tb + tb
                        if hc >= 2:
                            qst.wait(s, slot, st_val(hc - 2, tb))
                        inc(s.copy(out=qraw[:, slot * 512:(slot + 1) * 512],
                                   in_=ps[:, slot * 512:(slot + 1) * 512]), "aqk")

            @block.vector
            def _(v):
                for hc in range(32):
                    for tb in range(n_tb):
                        v.wait_ge(SEM["aqk"], a_base + hc * n_tb + tb + 1)
                        slot = (hc % 2) * n_tb + tb
                        qr = qraw[:, slot * 512:(slot + 1) * 512]
                        c_sl = cos_sb[:, tb * 512:(tb + 1) * 512]
                        s_sl = sin_sb[:, tb * 512:(tb + 1) * 512]
                        inc(v.tensor_scalar_mul(out=rotbuf[0:64, :], in0=qr[64:128, :],
                                                scalar1=-1.0), "hz")
                        inc(v.tensor_copy(out=rotbuf[64:128, :], in_=qr[0:64, :]), "hz")
                        v.wait_ge(SEM["hz"], CN["hz"])
                        inc(v.tensor_mul(out=t2buf[:, :], in0=rotbuf[:, :], in1=s_sl), "hz")
                        inc(v.tensor_mul(out=qr, in0=qr, in1=c_sl), "hz")
                        v.wait_ge(SEM["hz"], CN["hz"])
                        inc(v.tensor_add(out=qr, in0=qr, in1=t2buf[:, :]), "vqk")

            @block.sync
            def _(sy):
                for hc in range(32):
                    for tb in range(n_tb):
                        sy.wait_ge(SEM["vqk"], v_base + hc * n_tb + tb + 1)
                        slot = (hc % 2) * n_tb + tb
                        dst = bass.AP(scr_ap.tensor, (hc * 128) * n_t + tb * 512,
                                      [[n_t, 128], [1, 512]])
                        got = qst.dma(sy, slot, dst,
                                      qraw[:, slot * 512:(slot + 1) * 512])
                        assert got == st_val(hc, tb), (hc, tb, got)


        with nc.sbuf_tensor("xqnT", [128, 32 * NQ], BF16) as xqnT, \
             nc.sbuf_tensor("slab2", [128, 2 * 32 * 256], BF16) as slab2, \
             nc.sbuf_tensor("cosq_sb", [128, NQ], BF16) as cosq_sb, \
             nc.sbuf_tensor("sinq_sb", [128, NQ], BF16) as sinq_sb, \
             nc.sbuf_tensor("qraw2", [128, 4 * 512], BF16) as qraw2, \
             nc.sbuf_tensor("rot2", [128, 512], BF16) as rot2, \
             nc.sbuf_tensor("t2b2", [128, 512], BF16) as t2b2, \
             nc.psum_tensor("ps2", [128, 4 * 512], F32) as ps2:
            with nc.sbuf_tensor("xin", [128, 2 * D], F32) as xin, \
                 nc.sbuf_tensor("xnout", [128, 2 * D], BF16) as xnout, \
                 nc.sbuf_tensor("sqdump", [128, D], F32) as sqdump, \
                 nc.psum_tensor("ps0", [128, 8], F32) as ps0:

                def ln_src(tt):
                    if tt < NQT:
                        return xq_in[tt * 128:(tt + 1) * 128, :]
                    t = tt - NQT
                    return x_in[t * 128:(t + 1) * 128, :]

                def ln_dst(tt):
                    if tt < NQT:
                        return xnq_nat[tt * 128:(tt + 1) * 128, :]
                    t = tt - NQT
                    return xn_nat[t * 128:(t + 1) * 128, :]

                ld = {}
                st_v = {}
                xnq_marks = [None, None]
                xbr_q_chunks = {}

                def aln_base(tt):
                    return 3 * min(tt, NQT) + 2 * max(0, tt - NQT)

                @block.sync
                def _(sy):
                    c0.dma(sy, 0, biasrow_f[:, :], bias_in)
                    c0.dma(sy, 1, ws_sb[:, :], ws_in)
                    for tt in range(NTILE_LN):
                        if tt >= 2:
                            sy.wait_ge(SEM["vln"], 2 * (tt - 2) + 2)
                        ld[tt] = xin_c.dma(sy, tt % 2,
                                           xin[:, (tt % 2) * D:(tt % 2 + 1) * D],
                                           ln_src(tt))
                        if tt >= 1:
                            sy.wait_ge(SEM["vln"], 2 * (tt - 1) + 2)
                            st_v[tt - 1] = xst.dma(sy, (tt - 1) % 2, ln_dst(tt - 1),
                                                   xnout[:, ((tt - 1) % 2) * D:
                                                         ((tt - 1) % 2 + 1) * D])
                            if tt - 1 == 3:
                                xnq_marks[0] = xst.totals()
                            if tt - 1 == NQT - 1:
                                xnq_marks[1] = xst.totals()
                        if 5 <= tt <= 12:
                            h = 0 if tt <= 8 else 1
                            base_ci = (tt - (5 if h == 0 else 9)) * 8
                            if base_ci == 0:
                                for i, sem in enumerate(xst.sems):
                                    if xnq_marks[h][i]:
                                        sy.wait_ge(sem, xnq_marks[h][i])
                            for k in range(8):
                                ci = base_ci + k
                                slot = (h * 32 + ci) % 4
                                xbs.wait(sy, slot)
                                val = xbs.dma(
                                    sy, slot,
                                    xqnT[:, ci * NQ + h * 512:
                                         ci * NQ + (h + 1) * 512],
                                    xnq_nat[h * 512:(h + 1) * 512,
                                            ci * 128:(ci + 1) * 128],
                                    transpose=True)
                                xbr_q_chunks[(ci, h)] = (xbs.sems[slot], val)
                    sy.wait_ge(SEM["vln"], 2 * NTILE_LN)
                    st_v[NTILE_LN - 1] = xst.dma(
                        sy, (NTILE_LN - 1) % 2, ln_dst(NTILE_LN - 1),
                        xnout[:, ((NTILE_LN - 1) % 2) * D:((NTILE_LN - 1) % 2 + 1) * D])

                @block.gpsimd
                def _(g):
                    c0.dma(g, 2, ident_bf[:, :], ident_in)

                @block.vector
                def _(v):
                    v.memset(cpool[:, 0:1], -SHIFT)
                    v.memset(cpool[:, 1:2], 1e-5)
                    v.memset(ones_bf[:, :], 1.0)
                    v.memset(ones128[:, :], 1.0)
                    v.memset(mvbuf[:, :], 0.0)
                    inc(v.memset(onesf_row[:, :], 1.0), "vi")
                    c0.wait_all(v)
                    v.tensor_copy(out=biasrow_bf[:, :], in_=biasrow_f[:, :])
                    inc(v.reciprocal(out=wsinv_f[:, :], in_=ws_sb[:, :]), "hz")
                    v.wait_ge(SEM["hz"], CN["hz"])
                    v.tensor_scalar_mul(out=wsinv_row_bf[:, :], in0=onesf_row[:, :],
                                        scalar1=wsinv_f[0:1, 0:1])
                    v.wait_ge(SEM["ti"], 1)
                    inc(v.tensor_copy(out=ws_col[:, :], in_=ps0[:, 0:1]), "vi")
                    for tt in range(NTILE_LN):
                        xin_c.wait(v, tt % 2, ld[tt])
                        if tt >= 1:
                            v.wait_ge(SEM["vln"], 2 * tt)
                        sl = xin[:, (tt % 2) * D:(tt % 2 + 1) * D]
                        inc(v.reduce_sum(out=statbuf[:, 0:1], in_=sl, axis=AX.X), "hz")
                        v.wait_ge(SEM["hz"], CN["hz"])
                        v.wait_ge(SEM["aln"], aln_base(tt) + 1)
                        inc(v.tensor_scalar_mul(out=statbuf[:, 2:3], in0=statbuf[:, 0:1],
                                                scalar1=1.0 / D), "hz")
                        v.wait_ge(SEM["hz"], CN["hz"])
                        inc(v.tensor_mul(out=statbuf[:, 3:4], in0=statbuf[:, 2:3],
                                         in1=statbuf[:, 2:3]), "hz")
                        inc(v.tensor_scalar_mul(out=statbuf[:, 7:8], in0=statbuf[:, 1:2],
                                                scalar1=1.0 / D), "hz")
                        v.wait_ge(SEM["hz"], CN["hz"])
                        inc(v.tensor_sub(out=statbuf[:, 4:5], in0=statbuf[:, 7:8],
                                         in1=statbuf[:, 3:4]), "vln")
                        v.wait_ge(SEM["aln"], aln_base(tt) + 2)
                        inc(v.reciprocal(out=statbuf[:, 6:7], in_=statbuf[:, 5:6]), "hz")
                        v.wait_ge(SEM["hz"], CN["hz"])
                        if tt >= 2:
                            xst.wait(v, tt % 2, st_v[tt - 2])
                            if tt - 2 < NQT:
                                v.wait_ge(SEM["aln"], aln_base(tt - 2) + 3)
                        inc(v.tensor_scalar(out=xnout[:, (tt % 2) * D:(tt % 2 + 1) * D],
                                            in0=sl,
                                            scalar1=statbuf[:, 2:3],
                                            scalar2=statbuf[:, 6:7],
                                            op0=ALU.subtract, op1=ALU.mult), "vln")

                @block.tensor
                def _(t):
                    t.wait_ge(SEM["vi"], 1)
                    c0.wait_all(t)
                    inc(t.matmul(ps0[:, 0:1], lhsT=onesf_row[:, :], rhs=ws_sb[0:1, 0:1],
                                 start=True, stop=True), "ti")

                @block.scalar
                def _(s):
                    s.wait_ge(SEM["vi"], 1)
                    for tt in range(NTILE_LN):
                        base = aln_base(tt)
                        xin_c.wait(s, tt % 2, ld[tt])
                        if tt >= 1:
                            s.wait_ge(SEM["aln"], base)
                            s.wait_ge(SEM["vln"], 2 * (tt - 1) + 1)
                        sl = xin[:, (tt % 2) * D:(tt % 2 + 1) * D]
                        inc(s.activation(out=sqdump[:, :], in_=sl, func=AF.Square,
                                         accum_out=statbuf[:, 1:2]), "aln")
                        s.wait_ge(SEM["vln"], 2 * tt + 1)
                        inc(s.activation(out=statbuf[:, 5:6], in_=statbuf[:, 4:5],
                                         func=AF.Sqrt, bias=cpool[:, 1:2]), "aln")
                        if tt < NQT:
                            s.wait_ge(SEM["vln"], 2 * tt + 2)
                            inc(s.activation(out=sqdump[:, :],
                                             in_=xnout[:, (tt % 2) * D:(tt % 2 + 1) * D],
                                             func=AF.Copy, accum_out=rsq[:, tt:tt + 1]),
                                "aln")

            if upto >= 2:

                @block.gpsimd
                def _(g):
                    ctq.dma(g, 0, cosq_sb[:, :], cosq_in)
                    ctq.dma(g, 1, sinq_sb[:, :], sinq_in)

                @block.vector
                def _(v):
                    ctq.wait_all(v)

                emit_proj_T(wq_r, NQ, 2, cosq_sb, sinq_sb, qT_scr, slab2,
                            ps2, qraw2, rot2, t2b2, xqnT, xbr_q_chunks)

                xbs_q_tot = xbs.totals()

                @block.gpsimd
                def _(g):
                    for i, sem in enumerate(xbs.sems):
                        if xbs_q_tot[i]:
                            g.wait_ge(sem, xbs_q_tot[i])
                    for qt in range(NQT):
                        if qt >= 2:
                            xqs.wait(g, qt % 2)
                        src = bass.AP(xqnT, qt * 128,
                                      [[32 * NQ, 128], [NQ, 32], [1, 128]])
                        xqs.dma(g, qt % 2,
                                xnqT_scr[0:128, qt * 4096:(qt + 1) * 4096], src)
        P2_TQK = CN["tqk"]
        P2_AQK = CN["aqk"]

        try:
            if upto < 2:
                raise _StopEmit
            if upto < 3:
                raise _StopEmit
            with nc.sbuf_tensor("xnT", [128, 32 * S], BF16) as xnT, \
                 nc.sbuf_tensor("cosk_sb", [128, S], BF16) as cosk_sb, \
                 nc.sbuf_tensor("sink_sb", [128, S], BF16) as sink_sb:

                @block.gpsimd
                def _(g):
                    g.wait_ge(SEM["tqk"], P2_TQK)
                    qst.wait_all(g)
                    xst.wait_all(g)
                    xqs.wait_all(g)
                    ctk.dma(g, 0, cosk_sb[:, :], cosk_in)
                    ctk.dma(g, 1, sink_sb[:, :], sink_in)

                xbr_k_chunks = [None] * 32

                @block.sync
                def _(sy):
                    sy.wait_ge(SEM["tqk"], P2_TQK)
                    qst.wait_all(sy)
                    xst.wait_all(sy)
                    xqs.wait_all(sy)
                    for ci in range(32):
                        xbr.wait(sy, ci % 8)
                        val = xbr.dma(sy, ci % 8, xnT[:, ci * S:(ci + 1) * S],
                                      xn_nat[0:S, ci * 128:(ci + 1) * 128],
                                      transpose=True)
                        xbr_k_chunks[ci] = (xbr.sems[ci % 8], val)

                with nc.sbuf_tensor("slab3", [128, 2 * 32 * 256], BF16) as slab3, \
                     nc.sbuf_tensor("qraw3", [128, 8 * 512], BF16) as qraw3, \
                     nc.sbuf_tensor("rot3", [128, 512], BF16) as rot3, \
                     nc.sbuf_tensor("t2b3", [128, 512], BF16) as t2b3, \
                     nc.psum_tensor("ps3", [128, 8 * 512], F32) as ps3:

                    @block.vector
                    def _(v):
                        ctk.wait_all(v)

                    emit_proj_T(wk_r, S, 4, cosk_sb, sink_sb, kT_scr, slab3,
                                ps3, qraw3, rot3, t2b3, xnT, xbr_k_chunks)

                P3K_TQK = CN["tqk"]

                with nc.sbuf_tensor("vslab", [128, 32 * 512], BF16) as vslab, \
                     nc.sbuf_tensor("vev", [128, 8 * 512], BF16) as vev, \
                     nc.psum_tensor("psv", [128, 8 * 512], F32) as psv:

                    tv_base = CN["tqk"]
                    av_base = CN["aqk"]
                    tvh_base = CN["tvh"]
                    vq_base = qst.totals()
                    vslab_v = {}

                    def vst_val(halfidx, tt8):
                        return vq_base[tt8] + 16 * (halfidx + 1)

                    @block.gpsimd
                    def _(g):
                        qst.wait_all(g)
                        for vb in range(8):
                            for hf in range(2):
                                if hf == 0:
                                    if vb >= 1:
                                        g.wait_ge(SEM["tvh"], tvh_base + vb)
                                    else:
                                        g.wait_ge(SEM["tqk"], tv_base)
                                else:
                                    g.wait_ge(SEM["tqk"], tv_base + vb * 16)
                                src = bass.AP(wv_r.tensor,
                                              (vb * 32 + hf * 16) * 128 * 512,
                                              [[512, 128], [128 * 512, 16], [1, 512]])
                                vslab_v[(vb, hf)] = wsl.dma(
                                    g, hf, vslab[:, hf * 16 * 512:(hf + 1) * 16 * 512], src)

                    @block.tensor
                    def _(t):
                        for vb in range(8):
                            for half in range(2):
                                if half == 0:
                                    wsl.wait(t, 0, vslab_v[(vb, 0)])
                                if vb * 2 + half >= 1:
                                    t.wait_ge(SEM["aqk"], av_base + (vb * 2 + half) * 8)
                                for ci in range(32):
                                    if half == 0 and ci == 16:
                                        wsl.wait(t, 1, vslab_v[(vb, 1)])
                                    for tt8 in range(8):
                                        tt = half * 8 + tt8
                                        mm = t.matmul(
                                            psv[:, tt8 * 512:(tt8 + 1) * 512],
                                            lhsT=xnT[:, ci * S + tt * 128:
                                                     ci * S + tt * 128 + 128],
                                            rhs=vslab[:, ci * 512:(ci + 1) * 512],
                                            start=(ci == 0), stop=(ci == 31))
                                        if half == 1 and ci == 15 and tt8 == 7:
                                            inc(mm, "tvh")
                                        if ci == 31:
                                            inc(mm, "tqk")

                    @block.scalar
                    def _(s):
                        cnt = 0
                        for vb in range(8):
                            for half in range(2):
                                for tt8 in range(8):
                                    cnt += 1
                                    s.wait_ge(SEM["tqk"], tv_base + cnt)
                                    if vb * 2 + half >= 1:
                                        qst.wait(s, tt8, vst_val(vb * 2 + half - 1, tt8))
                                    inc(s.copy(out=vev[:, tt8 * 512:(tt8 + 1) * 512],
                                               in_=psv[:, tt8 * 512:(tt8 + 1) * 512]), "aqk")

                    @block.sync
                    def _(sy):
                        cnt = 0
                        for vb in range(8):
                            for half in range(2):
                                for tt8 in range(8):
                                    tt = half * 8 + tt8
                                    cnt += 1
                                    sy.wait_ge(SEM["aqk"], av_base + cnt)
                                    h0 = vb * 4
                                    dst = bass.AP(v_scr.tensor,
                                                  (h0 * NKT + tt) * 128 * 128,
                                                  [[128, 128], [NKT * 128 * 128, 4],
                                                   [1, 128]])
                                    got = qst.dma(
                                        sy, tt8, dst, vev[:, tt8 * 512:(tt8 + 1) * 512])
                                    assert got == vst_val(vb * 2 + half, tt8)
            P3_AQK = CN["aqk"]

            if upto < 4:
                raise _StopEmit
            with nc.sbuf_tensor("outhT", [128, H * NQ], BF16) as outhT:

                with nc.sbuf_tensor("mask_sb", [128, 16 * 128], BF16) as mask_sb, \
                     nc.sbuf_tensor("qh", [128, 2 * NQ], BF16) as qh, \
                     nc.sbuf_tensor("kh", [128, 2 * S], BF16) as kh, \
                     nc.sbuf_tensor("vh", [128, 2 * S], BF16) as vh, \
                     nc.sbuf_tensor("expT", [128, 2 * NQ], BF16) as expT, \
                     nc.sbuf_tensor("zinv_bc", [128, NQ], F32) as zinv_bc, \
                     nc.sbuf_tensor("pv_sb", [128, 2 * NQ], F32) as pv_sb, \
                     nc.sbuf_tensor("z_sb", [128, 2 * NQ], F32) as z_sb, \
                     nc.psum_tensor("psA", [128, 8 * 512], F32) as psA:
                    qv = {}
                    kv = {}
                    vv = {}

                    @block.gpsimd
                    def _(g):
                        g.wait_ge(SEM["aqk"], P3_AQK)
                        qst.wait_all(g)
                        msk = bass.AP(maskq_in.tensor, 0,
                                      [[128, 128], [128 * 128, 16], [1, 128]])
                        cmsk.dma(g, 0, mask_sb[:, :], msk)

                    @block.sync
                    def _(sy):
                        sy.wait_ge(SEM["aqk"], P3_AQK)
                        qst.wait_all(sy)
                        for h in range(H):
                            if h >= 2:
                                sy.wait_ge(SEM["tpv"], (h - 1) * NKT)
                            qv[h] = athq.dma(sy, h % 2,
                                             qh[:, (h % 2) * NQ:(h % 2 + 1) * NQ], qT_scr[h])
                            kv[h] = athk.dma(sy, h % 2,
                                             kh[:, (h % 2) * S:(h % 2 + 1) * S], kT_scr[h])
                            vsrc = bass.AP(v_scr.tensor, h * NKT * 128 * 128,
                                           [[128, 128], [128 * 128, NKT], [1, 128]])
                            vv[h] = athv.dma(sy, h % 2,
                                             vh[:, (h % 2) * S:(h % 2 + 1) * S], vsrc)

                    def strip_off(kt):
                        return (kt % 2) * NQ

                    def seg_list(kt):
                        lo = (kt // 2) * 128
                        out = []
                        for seg in range(0, NQ, 512):
                            a, b = max(seg, lo), seg + 512
                            if b > a:
                                out.append((a, b, seg))
                        return out

                    def last_kt_for_seg(seg):
                        last = -1
                        for kt in range(NKT):
                            if (kt // 2) * 128 < seg + 512:
                                last = kt
                        return last

                    LAST_KT = {seg: last_kt_for_seg(seg) for seg in range(0, NQ, 512)}

                    @block.tensor
                    def _(t):
                        for h in range(H):
                            athq.wait(t, h % 2, qv[h])
                            athk.wait(t, h % 2, kv[h])
                            athv.wait(t, h % 2, vv[h])
                            if h >= 1:
                                t.wait_ge(SEM["aexp"], h * NKT)
                                t.wait_ge(SEM["aoev"], 2 * h)
                            qsl = qh[:, (h % 2) * NQ:(h % 2 + 1) * NQ]
                            ksl = kh[:, (h % 2) * S:(h % 2 + 1) * S]
                            vsl = vh[:, (h % 2) * S:(h % 2 + 1) * S]

                            def scores(kt):
                                lo = (kt // 2) * 128
                                last = None
                                for sl0 in range(0, NQ - lo, 512):
                                    sl1 = min(sl0 + 512, NQ - lo)
                                    last = t.matmul(
                                        psA[:, strip_off(kt) + sl0:strip_off(kt) + sl1],
                                        lhsT=ksl[:, kt * 128:(kt + 1) * 128],
                                        rhs=qsl[:, lo + sl0:lo + sl1],
                                        start=True, stop=True)
                                inc(last, "ts4")

                            def zpv(kt):
                                lo = (kt // 2) * 128
                                mm = None
                                for (a, b, seg) in seg_list(kt):
                                    mm = t.matmul(
                                        psA[:, 3072 + a:3072 + b],
                                        lhsT=ones128[:, :],
                                        rhs=expT[:, strip_off(kt) + a - lo:
                                                 strip_off(kt) + b - lo],
                                        start=(kt == 0), stop=(kt == LAST_KT[seg]))
                                inc(mm, "tz")
                                for (a, b, seg) in seg_list(kt):
                                    mm = t.matmul(
                                        psA[:, 2048 + a:2048 + b],
                                        lhsT=vsl[:, kt * 128:(kt + 1) * 128],
                                        rhs=expT[:, strip_off(kt) + a - lo:
                                                 strip_off(kt) + b - lo],
                                        start=(kt == 0), stop=(kt == LAST_KT[seg]))
                                inc(mm, "tpv")

                            scores(0)
                            scores(1)
                            for kt in range(NKT):
                                t.wait_ge(SEM["aexp"], h * NKT + kt + 1)
                                zpv(kt)
                                if kt + 2 < NKT:
                                    scores(kt + 2)


                    @block.vector
                    def _(v):
                        cmsk.wait(v, 0)

                        def post_step(hp, j):
                            if hp < 0:
                                return
                            if j == 0:
                                v.wait_ge(SEM["aoev"], 2 * hp + 2)
                                if hp >= 1:
                                    v.wait_ge(SEM["vnat"], hp)
                            if j < NQT:
                                inc(v.reciprocal(
                                    out=zinv_bc[:, j * 128:(j + 1) * 128],
                                    in_=z_sb[:, (hp % 2) * NQ + j * 128:
                                             (hp % 2) * NQ + (j + 1) * 128]), "hz")
                            else:
                                v.wait_ge(SEM["hz"], CN["hz"])
                                inc(v.tensor_mul(out=outhT[:, hp * NQ:(hp + 1) * NQ],
                                                 in0=pv_sb[:, (hp % 2) * NQ:
                                                           (hp % 2 + 1) * NQ],
                                                 in1=zinv_bc[:, :]), "vnat")

                        for h in range(H):
                            for kt in range(NKT):
                                j0 = kt // 2
                                v.wait_ge(SEM["ts4"], h * NKT + kt + 1)
                                m_sl = mask_sb[:, (j0 * 2 + (kt % 2)) * 128:
                                               (j0 * 2 + (kt % 2) + 1) * 128]
                                inc(v.tensor_add(
                                    out=psA[:, strip_off(kt):strip_off(kt) + 128],
                                    in0=psA[:, strip_off(kt):strip_off(kt) + 128],
                                    in1=m_sl), "vm")
                                if kt % 2 == 1:
                                    post_step(h - 1, kt // 2)
                            post_step(h - 1, NQT)
                        for j in range(NQT + 1):
                            post_step(H - 1, j)

                    @block.scalar
                    def _(s):
                        for h in range(H):
                            for kt in range(NKT):
                                w = NQ - (kt // 2) * 128
                                s.wait_ge(SEM["vm"], h * NKT + kt + 1)
                                if kt >= 2:
                                    s.wait_ge(SEM["tpv"], h * NKT + kt - 1)
                                    s.wait_ge(SEM["tz"], h * NKT + kt - 1)
                                inc(s.activation(out=expT[:, strip_off(kt):strip_off(kt) + w],
                                                 in_=psA[:, strip_off(kt):strip_off(kt) + w],
                                                 func=AF.Exp, bias=cpool[:, 0:1],
                                                 scale=SM_SCALE), "aexp")
                            s.wait_ge(SEM["tpv"], (h + 1) * NKT)
                            if h >= 2:
                                s.wait_ge(SEM["vnat"], h - 1)
                            inc(s.copy(out=pv_sb[:, (h % 2) * NQ:(h % 2 + 1) * NQ],
                                       in_=psA[:, 2048:3072]), "aoev")
                            s.wait_ge(SEM["tz"], (h + 1) * NKT)
                            inc(s.copy(out=z_sb[:, (h % 2) * NQ:(h % 2 + 1) * NQ],
                                       in_=psA[:, 3072:4096]), "aoev")


                if upto >= 5:
                    with nc.sbuf_tensor("woslab", [128, 2 * 32 * 512], BF16) as woslab, \
                         nc.sbuf_tensor("sbf", [128, 2 * 512], BF16) as sbf, \
                         nc.psum_tensor("ps5", [128, 2 * 512], F32) as ps5:

                        two0 = CN["two"]
                        vwo0 = CN["vwo"]
                        sst_base = sst.totals()
                        woslab_v = {}

                        def sst_val(idx):
                            return sst_base[idx % 2] + 16 * (idx // 2 + 1)

                        @block.gpsimd
                        def _(g):
                            g.wait_ge(SEM["vnat"], H)
                            for ob in range(8):
                                if ob >= 2:
                                    g.wait_ge(SEM["two"], two0 + (ob - 1) * 8)
                                woslab_v[ob] = wsl.dma(
                                    g, ob % 2,
                                    woslab[:, (ob % 2) * 32 * 512:(ob % 2 + 1) * 32 * 512],
                                    w_slab_ap(wo_r, ob, 512))

                        @block.tensor
                        def _(t):
                            t.wait_ge(SEM["vnat"], H)
                            for ob in range(8):
                                wsl.wait(t, ob % 2, woslab_v[ob])
                                for qt in range(NQT):
                                    idx = ob * 8 + qt
                                    if idx >= 2:
                                        t.wait_ge(SEM["vwo"], vwo0 + idx - 1)
                                    for ci in range(32):
                                        mm = t.matmul(
                                            ps5[:, (idx % 2) * 512:(idx % 2 + 1) * 512],
                                            lhsT=outhT[:, ci * NQ + qt * 128:
                                                       ci * NQ + qt * 128 + 128],
                                            rhs=woslab[:, (ob % 2) * 32 * 512 + ci * 512:
                                                       (ob % 2) * 32 * 512 + (ci + 1) * 512],
                                            start=(ci == 0), stop=(ci == 31))
                                    inc(mm, "two")

                        @block.vector
                        def _(v):
                            for idx in range(64):
                                v.wait_ge(SEM["two"], two0 + idx + 1)
                                if idx >= 2:
                                    sst.wait(v, idx % 2, sst_val(idx - 2))
                                inc(v.tensor_copy(
                                    out=sbf[:, (idx % 2) * 512:(idx % 2 + 1) * 512],
                                    in_=ps5[:, (idx % 2) * 512:(idx % 2 + 1) * 512]), "vwo")

                        @block.sync
                        def _(sy):
                            for ob in range(8):
                                for qt in range(NQT):
                                    idx = ob * 8 + qt
                                    sy.wait_ge(SEM["vwo"], vwo0 + idx + 1)
                                    got = sst.dma(
                                        sy, idx % 2,
                                        s_scr[qt * 128:(qt + 1) * 128, ob * 512:(ob + 1) * 512],
                                        sbf[:, (idx % 2) * 512:(idx % 2 + 1) * 512])
                                    assert got == sst_val(idx)

            P5_TWO = CN["two"]
            P5_VWO = CN["vwo"]

            if upto < 6:
                raise _StopEmit
            with nc.sbuf_tensor("xnqT6", [128, 32 * NQ], BF16) as xnqT6, \
                 nc.sbuf_tensor("wft", [128, 2 * 32 * 512], BF16) as wft, \
                 nc.sbuf_tensor("gbuf", [128, 2 * 512], BF16) as gbuf, \
                 nc.sbuf_tensor("stile", [128, 2 * 512], BF16) as stile, \
                 nc.sbuf_tensor("xtile", [128, 2 * 512], F32) as xtile, \
                 nc.sbuf_tensor("sgbuf", [128, 512], F32) as sgbuf, \
                 nc.sbuf_tensor("obuf", [128, 2 * 512], F32) as obuf, \
                 nc.psum_tensor("ps6", [128, 2 * 512], F32) as ps6:

                vwf0 = CN["vwf"]
                ost_base = ost.totals()
                tff0 = CN["tff"]
                aff0 = CN["aff"]
                vout0 = CN["vout"]
                sti_v = {}
                xti_v = {}
                ost_v = {}
                wft_v = {}
                xq6_marks = {}

                @block.gpsimd
                def _(g):
                    g.wait_ge(SEM["two"], P5_TWO)
                    g.wait_ge(SEM["vwo"], P5_VWO)
                    sst.wait_all(g)
                    for ob in range(8):
                        if ob >= 2:
                            g.wait_ge(SEM["tff"], tff0 + (ob - 1) * 8)
                        wft_v[ob] = wsl.dma(
                            g, ob % 2,
                            wft[:, (ob % 2) * 32 * 512:(ob % 2 + 1) * 32 * 512],
                            w_slab_ap(wfT_r, ob, 512))

                @block.sync
                def _(sy):
                    sy.wait_ge(SEM["two"], P5_TWO)
                    sy.wait_ge(SEM["vwo"], P5_VWO)
                    sst.wait_all(sy)
                    for qt in range(NQT):
                        xbr.wait(sy, qt % 8)
                        xbr.dma(sy, qt % 8,
                                xnqT6[:, qt * 4096:(qt + 1) * 4096],
                                xnqT_scr[0:128, qt * 4096:(qt + 1) * 4096])
                        xq6_marks[qt] = xbr.totals()
                    for ob_s in range(8):
                        for qt in range(NQT):
                            idx = ob_s * 8 + qt
                            sy.wait_ge(SEM["vout"], vout0 + idx + 1)
                            ost_v[idx] = ost.dma(
                                sy, idx % 2,
                                out_d[qt * 128:(qt + 1) * 128,
                                      ob_s * 512:(ob_s + 1) * 512],
                                obuf[:, (idx % 2) * 512:(idx % 2 + 1) * 512])
                    ost.wait_all(sy)

                @block.scalar
                def _(s):
                    s.wait_ge(SEM["two"], P5_TWO)
                    s.wait_ge(SEM["vwo"], P5_VWO)
                    sst.wait_all(s)
                    for ob in range(8):
                        for qt in range(NQT):
                            idx = ob * 8 + qt
                            if idx >= 2:
                                s.wait_ge(SEM["vout"], vout0 + idx - 1)
                            sti_v[idx] = sti.dma(
                                s, idx % 2,
                                stile[:, (idx % 2) * 512:(idx % 2 + 1) * 512],
                                s_scr[qt * 128:(qt + 1) * 128, ob * 512:(ob + 1) * 512])
                            xti_v[idx] = xti.dma(
                                s, idx % 2,
                                xtile[:, (idx % 2) * 512:(idx % 2 + 1) * 512],
                                xq_in[qt * 128:(qt + 1) * 128, ob * 512:(ob + 1) * 512])
                            s.wait_ge(SEM["tff"], tff0 + idx + 1)
                            if idx == 0:
                                s.wait_ge(SEM["vwf"], vwf0 + 1)
                            inc(s.activation(out=gbuf[:, (idx % 2) * 512:(idx % 2 + 1) * 512],
                                             in_=ps6[:, (idx % 2) * 512:(idx % 2 + 1) * 512],
                                             func=AF.Silu, scale=ws_col[:, 0:1],
                                             bias=ucol[:, qt:qt + 1]), "aff")

                @block.vector
                def _(v):
                    v.wait_ge(SEM["vi"], 2)
                    v.wait_ge(SEM["aln"], 3 * NQT)
                    inc(v.tensor_scalar_mul(out=ucol[:, :], in0=rsq[:, :],
                                            scalar1=ws_col[:, 0:1]), "hz")
                    v.wait_ge(SEM["hz"], CN["hz"])
                    inc(v.tensor_scalar_mul(out=ucol[:, :], in0=ucol[:, :],
                                            scalar1=-1.5), "vwf")
                    for ob in range(8):
                        for qt in range(NQT):
                            idx = ob * 8 + qt
                            v.wait_ge(SEM["aff"], aff0 + idx + 1)
                            sti.wait(v, idx % 2, sti_v[idx])
                            xti.wait(v, idx % 2, xti_v[idx])
                            if idx >= 2:
                                ost.wait(v, idx % 2,
                                         ost_base[idx % 2] + 16 * ((idx - 2) // 2 + 1))
                            if idx >= 1:
                                v.wait_ge(SEM["vout"], vout0 + idx)
                            inc(v.tensor_mul(out=sgbuf[:, :],
                                             in0=gbuf[:, (idx % 2) * 512:(idx % 2 + 1) * 512],
                                             in1=stile[:, (idx % 2) * 512:(idx % 2 + 1) * 512]),
                                "hz")
                            v.wait_ge(SEM["hz"], CN["hz"])
                            inc(v.tensor_add(
                                out=obuf[:, (idx % 2) * 512:(idx % 2 + 1) * 512],
                                in0=sgbuf[:, :],
                                in1=xtile[:, (idx % 2) * 512:(idx % 2 + 1) * 512]), "vout")

                @block.tensor
                def _(t):
                    t.wait_ge(SEM["vwo"], P5_VWO)
                    for ob in range(8):
                        wsl.wait(t, ob % 2, wft_v[ob])
                        for qt in range(NQT):
                            idx = ob * 8 + qt
                            if ob == 0:
                                for i, sem in enumerate(xbr.sems):
                                    t.wait_ge(sem, xq6_marks[qt][i])
                            if idx >= 2:
                                t.wait_ge(SEM["aff"], aff0 + idx - 1)
                            for ci in range(32):
                                t.matmul(ps6[:, (idx % 2) * 512:(idx % 2 + 1) * 512],
                                         lhsT=xnqT6[:, qt * 4096 + ci * 128:
                                                    qt * 4096 + (ci + 1) * 128],
                                         rhs=wft[:, (ob % 2) * 32 * 512 + ci * 512:
                                                 (ob % 2) * 32 * 512 + (ci + 1) * 512],
                                         start=(ci == 0), stop=False)
                            mm = t.matmul(ps6[:, (idx % 2) * 512:(idx % 2 + 1) * 512],
                                          lhsT=wsinv_row_bf[0:1, 0:128],
                                          rhs=biasrow_bf[0:1, ob * 512:(ob + 1) * 512],
                                          start=False, stop=True)
                            inc(mm, "tff")


        except _StopEmit:
            pass

        if debug:
            @block.gpsimd
            def _(g):
                for ch in [c0, ctq, ctk, cmsk, xin_c, xst, wsl, xbr, xbs, qst,
                           athq, athk, athv, xqs, sst, sti, xti, ost]:
                    ch.wait_all(g)
                for nm in ["vln", "vqk", "aqk", "aoev", "vwo", "vout"]:
                    if CN[nm]:
                        g.wait_ge(SEM[nm], CN[nm])
                taps = [(dbg["dbg_mv"], mvbuf[:, :]),
                        (dbg["dbg_xn"], xn_nat[:, :]),
                        (dbg["dbg_xnq"], xnq_nat[:, :])]
                if upto >= 2:
                    taps.append((dbg["dbg_qt"],
                                 bass.AP(qT_scr.tensor, 0, [[NQ, H * 128], [1, NQ]])))
                if upto >= 3:
                    taps.append((dbg["dbg_kt"],
                                 bass.AP(kT_scr.tensor, 0, [[S, H * 128], [1, S]])))
                    taps.append((dbg["dbg_v"],
                                 bass.AP(v_scr.tensor, 0,
                                         [[128, H * NKT * 128], [1, 128]])))
                if upto >= 5:
                    taps.append((dbg["dbg_s"], s_scr[:, :]))
                for dst, src in taps:
                    if cdb.cnt[0]:
                        cdb.wait(g, 0)
                    cdb.dma(g, 0, dst, src)
                cdb.wait_all(g)

    return nc



def _host_tables():
    inv = 1.0 / (10000.0 ** (np.arange(0, HD, 2, dtype=np.float64) / HD))
    pos = np.arange(S, dtype=np.float64)
    fr = np.outer(inv, pos)
    cosT = np.cos(np.concatenate([fr, fr], axis=0)).astype(np.float32)
    sinT = np.sin(np.concatenate([fr, fr], axis=0)).astype(np.float32)
    return cosT, sinT


def _relayout(w, n_outer, width):
    return np.ascontiguousarray(
        w.reshape(32, 128, n_outer, width).transpose(2, 0, 1, 3))


def _masks(qts):
    m = np.zeros((NQT, 2, 128, 128), dtype=np.float32)
    p = np.arange(128)
    ecols = np.arange(128)
    for j, qt in enumerate(qts):
        for i in range(2):
            kt = 2 * j + i
            valid = (kt * 128 + p[:, None]) <= (qt * 128 + ecols[None, :])
            m[j, i] = np.where(valid, 0.0, -1e30)
    return m


def kernel(L0_attn_hidden_states, L0_attn_wq, L0_attn_wk, L0_attn_wv, L0_attn_wo,
           L0_attn_kq1, L0_attn_kq2, L0_attn_kk1, L0_attn_kk2, L0_attn_scalingD,
           L0_attn_interaction_k, L0_attn_scalingD2, L0_ffn_weight_2bit,
           L0_ffn_weight_scale, L0_ffn_bias):
    x = np.ascontiguousarray(np.asarray(L0_attn_hidden_states, dtype=np.float32))
    wq = np.asarray(L0_attn_wq, dtype=np.float32)
    wk = np.asarray(L0_attn_wk, dtype=np.float32)
    wv = np.asarray(L0_attn_wv, dtype=np.float32)
    wo = np.asarray(L0_attn_wo, dtype=np.float32)
    wf = np.asarray(L0_ffn_weight_2bit, dtype=np.int32)
    ws = np.ascontiguousarray(np.asarray(L0_ffn_weight_scale, dtype=np.float32).reshape(1, 1))
    bias = np.ascontiguousarray(np.asarray(L0_ffn_bias, dtype=np.float32).reshape(1, D))

    cosT, sinT = _host_tables()
    ident = np.eye(128, dtype=np.float32)
    wq_r = _relayout(wq, 16, 256)
    wk_r = _relayout(wk, 16, 256)
    wv_r = _relayout(wv, 8, 512)
    wo_r = _relayout(wo, 8, 512)
    wfT_r = _relayout(wf.T, 8, 512)
    masks = [_masks(QT_SETS[0]), _masks(QT_SETS[1])]
    rows_sets = [np.concatenate([np.arange(qt * 128, (qt + 1) * 128) for qt in qs])
                 for qs in QT_SETS]

    in_maps = []
    for c in range(8):
        beta, par = c // 2, c % 2
        rows = rows_sets[par]
        in_maps.append({
            "x": np.ascontiguousarray(x[beta]),
            "xq": np.ascontiguousarray(x[beta][rows]),
            "wq_r": wq_r, "wk_r": wk_r, "wv_r": wv_r, "wo_r": wo_r,
            "wfT_r": wfT_r, "bias": bias, "ws": ws,
            "cosq": np.ascontiguousarray(cosT[:, rows]),
            "sinq": np.ascontiguousarray(sinT[:, rows]),
            "cosk": cosT, "sink": sinT,
            "maskq": masks[par], "ident": ident,
        })

    if "nc" not in _NC_CACHE:
        _NC_CACHE["nc"] = build(False)
    nc = _NC_CACHE["nc"]
    res = run_bass_kernel_spmd(nc, in_maps, core_ids=list(range(8)))

    out = np.empty((4, S, D), dtype=np.float32)
    for c in range(8):
        beta, par = c // 2, c % 2
        out[beta][rows_sets[par]] = np.asarray(res.results[c]["out"], dtype=np.float32)
    return out
